# version 30
# baseline (speedup 1.0000x reference)
import numpy as np
import ml_dtypes
from contextlib import ExitStack

import concourse.bass as bass
import concourse.mybir as mybir
from concourse.bass_utils import run_bass_kernel_spmd
from concourse.alu_op_type import AluOpType as OP

F32 = mybir.dt.float32
BF16 = mybir.dt.bfloat16
U8 = mybir.dt.uint8
AF = mybir.ActivationFunctionType
AX = mybir.AxisListType.X
BF = ml_dtypes.bfloat16

D = 1024
NB = 512
NMB = 4
NDK = 8
TST = 512
NTC = 4
NST = 8
TC = NST * TST
F0 = 4096
F1 = 4096
NF0 = F0 // 128
NF1 = F1 // 128
CONV_Q = 1024


def host_consts():
    d = np.arange(D)[:, None].astype(np.float64)
    m = np.arange(NB)[None, :].astype(np.float64)
    ang = 2 * np.pi * d * m / D
    cosm = np.cos(ang)
    sinm = -np.sin(ang)
    sinm[:, 0] = (-1.0) ** np.arange(D)
    j = np.arange(D)[None, :].astype(np.float64)
    mm = np.arange(NB)[:, None].astype(np.float64)
    angG = 2 * np.pi * j * mm / D
    c = 1.0 / (D * np.sqrt(D))
    GR = 2 * c * np.cos(angG)
    GR[0, :] = c
    GI = -2 * c * np.sin(angG)
    GI[0, :] = c * ((-1.0) ** np.arange(D))
    G = np.concatenate([GR, GI], axis=0)
    return (np.ascontiguousarray(cosm.astype(np.float32).astype(BF)),
            np.ascontiguousarray(sinm.astype(np.float32).astype(BF)),
            np.ascontiguousarray(G.astype(np.float32).astype(BF)))


def build_nc(dbg_mode=False):
    nc = bass.Bass("TRN2", target_bir_lowering=False)

    x_own = nc.dram_tensor("x_own", [TC, D], F32, kind="ExternalInput")
    x_extra = nc.dram_tensor("x_extra", [TC, D], F32, kind="ExternalInput")
    Wmat = {p: nc.dram_tensor("W" + p, [D, D], F32, kind="ExternalInput")
            for p in ("k", "v", "q")}
    bvec = {p: nc.dram_tensor("b" + p, [D], F32, kind="ExternalInput")
            for p in ("k", "v", "q")}
    k0u = nc.dram_tensor("k0u", [2 * D, F0], U8, kind="ExternalInput")
    k1u = nc.dram_tensor("k1u", [F0, F1], U8, kind="ExternalInput")
    k2u = nc.dram_tensor("k2u", [F1, D], U8, kind="ExternalInput")
    s0r = nc.dram_tensor("s0r", [128, 1], F32, kind="ExternalInput")
    s1r = nc.dram_tensor("s1r", [128, 1], F32, kind="ExternalInput")
    s2r = nc.dram_tensor("s2r", [128, 1], F32, kind="ExternalInput")
    maskr = nc.dram_tensor("maskr", [128, 1], F32, kind="ExternalInput")
    cosm_d = nc.dram_tensor("cosm", [D, NB], BF16, kind="ExternalInput")
    sinm_d = nc.dram_tensor("sinm", [D, NB], BF16, kind="ExternalInput")
    Gm = nc.dram_tensor("Gm", [D, D], BF16, kind="ExternalInput")
    identm = nc.dram_tensor("identm", [128, 128], BF16, kind="ExternalInput")

    ikind = "ExternalOutput" if dbg_mode else "Internal"
    xn_dram = nc.dram_tensor("xn_dram", [TC, D], BF16, kind=ikind)
    xne_dram = nc.dram_tensor("xne_dram", [TC, D], BF16, kind=ikind)
    partial_dram = nc.dram_tensor("partial_dram", [TC, D], BF16, kind=ikind)
    w1s = nc.dram_tensor("w1s", [NF1, 128, F0], U8, kind=ikind)
    k0s = nc.dram_tensor("k0s", [D, F0], BF16, kind=ikind)
    out = nc.dram_tensor("out", [TC, D], F32, kind="ExternalOutput")

    es = ExitStack()

    def mk_sb(stack, name, shape, dt):
        return stack.enter_context(nc.sbuf_tensor(name, shape, dt))

    class Ctr:
        __slots__ = ("h", "n", "name")

        def __init__(self, name):
            self.h = es.enter_context(nc.semaphore(name))
            self.n = 0
            self.name = name

        def inc(self, inst, k=1):
            inst.then_inc(self.h, k)
            self.n += k
            return self.n

    marks = {}

    def wge(eng, key, ctr, val):
        if val <= 0:
            return
        if marks.get((key, ctr.name), 0) >= val:
            return
        eng.wait_ge(ctr.h, val)
        marks[(key, ctr.name)] = val

    track = {}

    def dep_wait(eng, ekey, ins, outs):
        need = {}
        for t in ins:
            e = track.get(t)
            if e and e[0] is not None:
                c, v = e[0]
                need[c] = max(need.get(c, 0), v)
        for t in outs:
            e = track.get(t)
            if e:
                if e[0] is not None:
                    c, v = e[0]
                    need[c] = max(need.get(c, 0), v)
                for c, v in e[1].items():
                    need[c] = max(need.get(c, 0), v)
        for c, v in need.items():
            wge(eng, ekey, c, v)

    def record(ctr, val, ins, outs):
        for t in ins:
            e = track.setdefault(t, [None, {}])
            e[1][ctr] = max(e[1].get(ctr, 0), val)
        for t in outs:
            track[t] = [(ctr, val), {}]

    def step(eng, ekey, ctr, ins, outs, emit_fn):
        dep_wait(eng, ekey, ins, outs)
        inst = emit_fn()
        v = ctr.inc(inst)
        record(ctr, v, ins, outs)
        return v

    def dstep(eng, ekey, ctr, ins, outs, emit_fn):
        dep_wait(eng, ekey, ins, outs)
        inst = emit_fn()
        v = ctr.inc(inst, 16)
        record(ctr, v, ins, outs)
        return v

    s_x = [Ctr("s_x0"), Ctr("s_x1")]
    s_ld = Ctr("s_ld")
    s_bld = Ctr("s_bld")
    s_wld = Ctr("s_wld")
    s_wldb = Ctr("s_wldb")
    s_cv = [Ctr(f"s_cv{i}") for i in range(4)]
    s_xnw = Ctr("s_xnw")
    s_xnw2 = Ctr("s_xnw2")
    s_xnt = [Ctr("s_xnt0"), Ctr("s_xnt1")]
    s_cw = [Ctr("s_cw0"), Ctr("s_cw1")]
    s_k0d = Ctr("s_k0d")
    s_w1 = [Ctr("s_w1a"), Ctr("s_w1b"), Ctr("s_w1c")]
    s_w2 = [Ctr("s_w2a"), Ctr("s_w2b"), Ctr("s_w2c"), Ctr("s_w2d")]
    s_pw = [Ctr(f"s_pw{i}") for i in range(4)]
    s_bx = [Ctr(f"s_bx{i}") for i in range(4)]
    s_pr = [Ctr(f"s_pr{i}") for i in range(4)]
    s_ow = [Ctr(f"s_ow{i}") for i in range(4)]
    s_pe = Ctr("s_pe")
    s_act = Ctr("s_act")
    s_dve = Ctr("s_dve")
    s_pool = Ctr("s_pool")

    xn_f = [mk_sb(es, f"xn_f{i}", [128, NDK * TST], BF16) for i in range(2)]
    part_t = [mk_sb(es, f"part_t{i}", [128, D], BF16) for i in range(2)]
    s_scales = mk_sb(es, "s_scales", [128, 6], F32)
    cst_t = mk_sb(es, "cst_t", [128, 3], F32)

    pd = [es.enter_context(nc.psum_tensor(f"pd{i}", [128, 1024], F32))
          for i in range(4)]

    class Region:
        __slots__ = ("ap", "banks")

        def __init__(self, ap, banks):
            self.ap = ap
            self.banks = banks

    bank_rel = [None] * 8
    singles = [Region(pd[2][:, 0:512], [4]), Region(pd[2][:, 512:1024], [5]),
               Region(pd[3][:, 0:512], [6]), Region(pd[3][:, 512:1024], [7])]
    doubles = [Region(pd[0][:, :], [0, 1]), Region(pd[1][:, :], [2, 3]),
               Region(pd[2][:, :], [4, 5]), Region(pd[3][:, :], [6, 7])]

    def pacq(te, reg):
        for b in reg.banks:
            if bank_rel[b] is not None:
                c, v = bank_rel[b]
                wge(te, "pe", c, v)

    def prel(reg, ctr, val):
        for b in reg.banks:
            bank_rel[b] = (ctr, val)

    sidx = [0]
    didx = [0]

    block = es.enter_context(nc.Block())

    es_a = ExitStack()
    sA = {p + ri: mk_sb(es_a, "A" + p + ri, [128, NDK * NB], BF16)
          for p in ("k", "v", "q") for ri in ("R", "I")}
    sG = mk_sb(es_a, "sG", [128, NDK * D], BF16)
    sbias = {p + ri: mk_sb(es_a, "b" + p + ri, [1, NB], BF16)
             for p in ("k", "v", "q") for ri in ("R", "I")}
    ones1_t = mk_sb(es_a, "ones1_t", [1, TST], BF16)
    ident_t = mk_sb(es_a, "ident_t", [128, 128], BF16)
    carryR = mk_sb(es_a, "carryR", [128, NMB], F32)
    carryI = mk_sb(es_a, "carryI", [128, NMB], F32)
    acc = {("R", i): mk_sb(es_a, f"accR{i}", [128, NMB], F32) for i in range(2)}
    acc.update({("I", i): mk_sb(es_a, f"accI{i}", [128, NMB], F32) for i in range(2)})

    es_a0 = ExitStack()
    wtmp2 = [mk_sb(es_a0, f"wtmp{i}", [128, NDK * D], BF16) for i in range(2)]
    cosm_t = mk_sb(es_a0, "cosm_t", [128, NDK * NB], BF16)
    sinm_t = mk_sb(es_a0, "sinm_t", [128, NDK * NB], BF16)
    bcol = {p: mk_sb(es_a0, "bcol_" + p, [128, NDK], BF16) for p in ("k", "v", "q")}

    @block.sync
    def _(sync):
        dstep(sync, "sp", s_ld, [], ["sG"],
              lambda: sync.dma_start(sG[:], Gm.ap().rearrange("(k p) j -> p k j", p=128)))
        dstep(sync, "sp", s_ld, [], ["scl0"],
              lambda: sync.dma_start(s_scales[:, 0:1], s0r.ap()))
        dstep(sync, "sp", s_ld, [], ["scl1"],
              lambda: sync.dma_start(s_scales[:, 1:2], s1r.ap()))
        dstep(sync, "sp", s_ld, [], ["scl2"],
              lambda: sync.dma_start(s_scales[:, 2:3], s2r.ap()))
        dstep(sync, "sp", s_ld, [], ["scl3"],
              lambda: sync.dma_start(s_scales[:, 3:4], maskr.ap()))
        dstep(sync, "sp", s_ld, [], ["cosm"],
              lambda: sync.dma_start(cosm_t[:], cosm_d.ap().rearrange("(k p) m -> p k m", p=128)))
        dstep(sync, "sp", s_ld, [], ["sinm"],
              lambda: sync.dma_start(sinm_t[:], sinm_d.ap().rearrange("(k p) m -> p k m", p=128)))
        dstep(sync, "sp", s_ld, [], ["ident"],
              lambda: sync.dma_start(ident_t[:], identm.ap()))
        for kk in ("sG", "scl0", "scl1", "scl2", "scl3", "cosm", "sinm", "ident"):
            track[kk] = [(s_ld, s_ld.n), {}]

    @block.gpsimd
    def _(g):
        for p in ("k", "v", "q"):
            dstep(g, "pool", s_bld, [], ["bcol" + p],
                  lambda p=p: g.dma_start(bcol[p][:],
                                          bvec[p].ap().rearrange("(k p) -> p k", p=128),
                                          allow_slow_non_contiguous=True))
        for p in ("k", "v", "q"):
            track["bcol" + p] = [(s_bld, s_bld.n), {}]

    @block.vector
    def _(v):
        step(v, "dve", s_dve, ["scl0"], ["scl0"],
             lambda: v.tensor_scalar(s_scales[:, 0:1], s_scales[:, 0:1], 2.0, None, op0=OP.mult))
        step(v, "dve", s_dve, ["scl1"], ["scl1"],
             lambda: v.tensor_scalar(s_scales[:, 1:2], s_scales[:, 1:2], 2.0, None, op0=OP.mult))
        step(v, "dve", s_dve, ["scl2"], ["scl2"],
             lambda: v.tensor_scalar(s_scales[:, 2:3], s_scales[:, 2:3], 2.0, None, op0=OP.mult))
        step(v, "dve", s_dve, ["scl1"], ["scl4"],
             lambda: v.tensor_scalar(s_scales[:, 4:5], s_scales[:, 1:2], 2.0, None, op0=OP.mult))
        step(v, "dve", s_dve, ["scl2"], ["scl5"],
             lambda: v.tensor_scalar(s_scales[:, 5:6], s_scales[:, 2:3], 2.0, None, op0=OP.mult))
        for ri in "RI":
            for i in range(2):
                step(v, "dve", s_dve, [], [f"acc{ri}{i}"],
                     lambda ri=ri, i=i: v.memset(acc[(ri, i)][:], 0.0))
        step(v, "dve", s_dve, [], ["carryR"], lambda: v.memset(carryR[:], 0.0))
        step(v, "dve", s_dve, [], ["carryI"], lambda: v.memset(carryI[:], 0.0))
        step(v, "dve", s_dve, [], ["cst0"], lambda: v.memset(cst_t[:, 0:1], 1e-5))
        step(v, "dve", s_dve, [], ["cst1"], lambda: v.memset(cst_t[:, 1:2], -0.5))
        step(v, "dve", s_dve, [], ["cst2"], lambda: v.memset(cst_t[:, 2:3], 0.5))
        step(v, "dve", s_dve, [], ["ones1"], lambda: v.memset(ones1_t[:], 1.0))

    for pi, p in enumerate(("k", "v", "q")):
        wb = pi % 2
        wtmp = wtmp2[wb]
        wkey = f"wtmp{wb}"

        wsem = s_wld if wb == 0 else s_wldb

        @block.gpsimd
        def _(g, p=p, wtmp=wtmp, wkey=wkey, wsem=wsem):
            dstep(g, "pool", wsem, [], [wkey],
                  lambda: g.dma_start(wtmp[:], Wmat[p].ap().rearrange("(k p) d -> p k d", p=128)))

        for ri, trig_t, trig_key in (("R", cosm_t, "cosm"), ("I", sinm_t, "sinm")):
            for dm in range(NDK):
                reg = singles[sidx[0] % 4]
                sidx[0] += 1

                @block.tensor
                def _(te, p=p, dm=dm, reg=reg, trig_t=trig_t, trig_key=trig_key,
                      wtmp=wtmp, wkey=wkey):
                    dep_wait(te, "pe", [wkey, trig_key], [])
                    pacq(te, reg)
                    for ek in range(NDK):
                        inst = te.matmul(
                            reg.ap,
                            wtmp[:, ek * D + dm * 128: ek * D + (dm + 1) * 128],
                            trig_t[:, ek * NB:(ek + 1) * NB],
                            start=(ek == 0), stop=(ek == NDK - 1))
                    v = s_pe.inc(inst)
                    record(s_pe, v, [wkey, trig_key], [])
                pe_done = s_pe.n

                @block.scalar
                def _(sc, p=p, ri=ri, dm=dm, reg=reg, pe_done=pe_done):
                    wge(sc, "act", s_pe, pe_done)
                    step(sc, "act", s_act, [], ["A" + p + ri],
                         lambda: sc.copy(sA[p + ri][:, dm * NB:(dm + 1) * NB], reg.ap))
                prel(reg, s_act, s_act.n)

    for p in ("k", "v", "q"):
        for ri, trig_t, trig_key in (("R", cosm_t, "cosm"), ("I", sinm_t, "sinm")):
            for mb in range(NMB):
                reg = singles[sidx[0] % 4]
                sidx[0] += 1

                @block.tensor
                def _(te, p=p, mb=mb, reg=reg, trig_t=trig_t, trig_key=trig_key):
                    dep_wait(te, "pe", [trig_key, "bcol" + p], [])
                    pacq(te, reg)
                    for ek in range(NDK):
                        inst = te.matmul(
                            reg.ap[0:1, mb * 128:(mb + 1) * 128],
                            bcol[p][:, ek:ek + 1],
                            trig_t[:, ek * NB + mb * 128: ek * NB + (mb + 1) * 128],
                            start=(ek == 0), stop=(ek == NDK - 1))
                    v = s_pe.inc(inst)
                    record(s_pe, v, [trig_key, "bcol" + p], [])
                pe_done = s_pe.n

                @block.scalar
                def _(sc, p=p, ri=ri, mb=mb, reg=reg, pe_done=pe_done):
                    wge(sc, "act", s_pe, pe_done)
                    step(sc, "act", s_act, [], ["b" + p + ri],
                         lambda: sc.copy(sbias[p + ri][0:1, mb * 128:(mb + 1) * 128],
                                         reg.ap[0:1, mb * 128:(mb + 1) * 128]))
                prel(reg, s_act, s_act.n)

    a0_pe_done = s_pe.n
    a0_act_done = s_act.n
    es_a0.close()

    x_t = [mk_sb(es_a, f"x_t{i}", [128, NTC * D], BF16) for i in range(2)]
    xn_t = mk_sb(es_a, "xn_t", [128, NTC * D], BF16)
    stats_t = mk_sb(es_a, "stats_t", [128, 48], F32)
    mv_t = mk_sb(es_a, "mv_t", [128, 2 * NTC], F32)
    rstd_t = mk_sb(es_a, "rstd_t", [128, NTC], F32)
    negmr_t = mk_sb(es_a, "negmr_t", [128, NTC], F32)
    mean_t = mk_sb(es_a, "mean_t", [128, NTC], F32)
    var_t = mk_sb(es_a, "var_t", [128, NTC], F32)
    lnv_t = mk_sb(es_a, "lnv_t", [128, NTC], F32)
    proj = {p + ri: mk_sb(es_a, "p" + p + ri, [128, NMB * TST], BF16)
            for p in ("k", "v", "q") for ri in ("R", "I")}
    ta = [mk_sb(es_a, f"ta{m}", [128, TST], BF16) for m in range(NMB)]
    tb = [mk_sb(es_a, f"tb{m}", [128, TST], BF16) for m in range(NMB)]
    tc = [mk_sb(es_a, f"tc{m}", [128, TST], BF16) for m in range(NMB)]
    td = [mk_sb(es_a, f"td{m}", [128, TST], BF16) for m in range(NMB)]
    kv_t = {"R": mk_sb(es_a, "kvR_t", [128, NMB * TST], BF16),
            "I": mk_sb(es_a, "kvI_t", [128, NMB * TST], BF16)}
    cum_t = kv_t
    z_t2 = [{"R": mk_sb(es_a, f"zR_t{i}", [128, NMB * TST], BF16),
             "I": mk_sb(es_a, f"zI_t{i}", [128, NMB * TST], BF16)} for i in range(2)]
    attn_t = [mk_sb(es_a, f"attn_t{i}", [128, D], BF16) for i in range(2)]
    rred = {nm: mk_sb(es_a, "r" + nm, [128, NMB], F32)
            for nm in ("ta", "tb", "tc", "td")}
    racc_t = mk_sb(es_a, "racc_t", [128, 2 * NMB], F32)
    S_t = mk_sb(es_a, "S_t", [128, NTC], F32)
    lnS_t = mk_sb(es_a, "lnS_t", [128, NTC + 1], F32)
    sqS_t = mk_sb(es_a, "sqS_t", [128, NTC], F32)
    inv_t = mk_sb(es_a, "inv_t", [128, NTC], F32)
    ones_t = mk_sb(es_a, "ones_t", [128, TST], BF16)
    conv_t = [mk_sb(es_a, f"conv_t{i}", [128, CONV_Q], BF16) for i in range(4)]
    conv_o = [mk_sb(es_a, f"conv_o{i}", [128, CONV_Q], BF16) for i in range(2)]

    @block.vector
    def _(v):
        step(v, "dve", s_dve, [], ["ones"], lambda: v.memset(ones_t[:], 1.0))

    conv_tasks = ([("k0", i) for i in range(4 * NDK)]
                  + [("w1", k) for k in range(NF0)])
    conv_i = [0]

    def emit_conv_tasks(n):
        for _ in range(n):
            if conv_i[0] >= len(conv_tasks):
                return
            kind, idx = conv_tasks[conv_i[0]]
            conv_i[0] += 1
            cb = conv_i[0] % 2
            c4 = (2 * conv_i[0]) % 4
            if kind == "w1":
                @block.sync
                def _(sync, k=idx):
                    dstep(sync, "sp", s_cw[k % 2], [], ["w1s"],
                          lambda: sync.dma_start(
                              w1s.ap()[:, :, k * 128:(k + 1) * 128]
                              .rearrange("c p m -> p c m"),
                              k1u.ap()[k * 128:(k + 1) * 128, :]))
            else:
                kd, hh = idx // 4, idx % 4

                @block.gpsimd
                def _(g, kd=kd, hh=hh, c4=c4):
                    dstep(g, "pool", s_cv[c4], [], [f"cv{c4}"],
                          lambda: g.dma_start(
                              conv_t[c4][:],
                              k0u.ap()[kd * 128:(kd + 1) * 128,
                                       hh * CONV_Q:(hh + 1) * CONV_Q]))
                    dstep(g, "pool", s_cv[c4 + 1], [], [f"cv{c4 + 1}"],
                          lambda: g.dma_start(
                              conv_t[c4 + 1][:],
                              k0u.ap()[D + kd * 128: D + (kd + 1) * 128,
                                       hh * CONV_Q:(hh + 1) * CONV_Q]))

                @block.gpsimd
                def _(g, cb=cb, c4=c4):
                    step(g, "pool", s_pool, [f"cv{c4}", f"cv{c4 + 1}"], [f"cvo{cb}"],
                         lambda: g.tensor_tensor(conv_o[cb][:], conv_t[c4][:],
                                                 conv_t[c4 + 1][:], op=OP.subtract))

                @block.sync
                def _(sync, kd=kd, hh=hh, cb=cb):
                    dstep(sync, "sp", s_cw[cb], [f"cvo{cb}"], ["k0s"],
                          lambda: sync.dma_start(
                              k0s.ap()[kd * 128:(kd + 1) * 128,
                                       hh * CONV_Q:(hh + 1) * CONV_Q],
                              conv_o[cb][:]))

    def emit_xload(st, xd):
        xb = st % 2
        xk = f"x{xb}"

        @block.gpsimd
        def _(g, st=st, xb=xb, xk=xk):
            wge(g, "pool", s_pe, a0_pe_done)
            wge(g, "pool", s_act, a0_act_done)
            dstep(g, "pool", s_x[xb], [], [xk],
                  lambda: g.dma_start(
                      x_t[xb][:],
                      xd.ap()[st * TST:(st + 1) * TST, :].rearrange("(j p) d -> p j d", p=128)))

    def emit_prep_pre(st):
        xb = st % 2
        xk = f"x{xb}"

        @block.vector
        def _(v, xb=xb, xk=xk):
            for jj in range(2 * NTC):
                step(v, "dve", s_dve, [xk], [f"stats{jj}"],
                     lambda jj=jj: v.bn_stats(stats_t[:, jj * 6:(jj + 1) * 6],
                                              x_t[xb][:, jj * 512:(jj + 1) * 512]))
            for j in range(NTC):
                step(v, "dve", s_dve, [f"stats{2 * j}", f"stats{2 * j + 1}"], ["mv"],
                     lambda j=j: v.bn_aggr(mv_t[:, 2 * j:2 * j + 2],
                                           stats_t[:, 12 * j:12 * (j + 1)]))
            step(v, "dve", s_dve, ["mv"], ["meanv"],
                 lambda: v.tensor_copy(mean_t[:], bass.AP(mv_t, 0, [[2 * NTC, 128], [2, NTC]])))
            step(v, "dve", s_dve, ["mv"], ["varv"],
                 lambda: v.tensor_copy(var_t[:], bass.AP(mv_t, 1, [[2 * NTC, 128], [2, NTC]])))

    def mk_prep_finish(st, xnd, xnd_key):
        def finish(st=st, xnd=xnd, xnd_key=xnd_key):
            xb = st % 2
            xk = f"x{xb}"

            @block.scalar
            def _(sc):
                step(sc, "act", s_act, ["varv", "cst0"], ["lnx"],
                     lambda: sc.activation(lnv_t[:], var_t[:], AF.Ln, bias=cst_t[:, 0:1]))
                step(sc, "act", s_act, ["lnx", "cst1"], ["rstd"],
                     lambda: sc.activation(rstd_t[:], lnv_t[:], AF.Exp, scale=cst_t[:, 1:2]))

            @block.vector
            def _(v, xb=xb, xk=xk):
                step(v, "dve", s_dve, ["meanv", "rstd"], ["negmr"],
                     lambda: v.scalar_tensor_tensor(negmr_t[:], mean_t[:], -1.0, rstd_t[:],
                                                    op0=OP.mult, op1=OP.mult))
                for jj in range(NTC):
                    step(v, "dve", s_dve, [xk, "negmr", "rstd"], [f"xn{jj}"],
                         lambda jj=jj: v.tensor_scalar(xn_t[:, jj * D:(jj + 1) * D],
                                                       x_t[xb][:, jj * D:(jj + 1) * D],
                                                       rstd_t[:, jj:jj + 1],
                                                       negmr_t[:, jj:jj + 1],
                                                       op0=OP.mult, op1=OP.add))

            fb = st % 2
            fkeys = [f"xnf{fb}_{dk}" for dk in range(NDK)]
            xnk = [f"xn{jj}" for jj in range(NTC)]

            if xnd is None:
                for d0 in range(0, NDK, 4):
                    pend = []
                    for dk in range(d0, d0 + 4):
                        reg = singles[sidx[0] % 4]
                        sidx[0] += 1

                        @block.tensor
                        def _(te, dk=dk, reg=reg):
                            dep_wait(te, "pe", xnk + ["ident"], [])
                            pacq(te, reg)
                            for c in range(NTC):
                                inst = te.matmul(
                                    reg.ap[:, c * 128:(c + 1) * 128],
                                    xn_t[:, c * D + dk * 128: c * D + (dk + 1) * 128],
                                    ident_t[:],
                                    start=True, stop=True)
                            v = s_pe.inc(inst)
                            record(s_pe, v, xnk + ["ident"], [])
                        pend.append((dk, reg, s_pe.n))

                    @block.scalar
                    def _(sc, pend=pend, fb=fb, fkeys=fkeys):
                        for dk, reg, pe_done in pend:
                            wge(sc, "act", s_pe, pe_done)
                            step(sc, "act", s_act, [], [fkeys[dk]],
                                 lambda dk=dk, reg=reg:
                                 sc.copy(xn_f[fb][:, dk * TST:(dk + 1) * TST], reg.ap))
                            prel(reg, s_act, s_act.n)
                a_fk[st] = fkeys
                return

            @block.sync
            def _(sync, st=st, fb=fb, fkeys=fkeys):
                dstep(sync, "sp", s_xnw, xnk, [xnd_key + "0"],
                      lambda: sync.dma_start(
                          xnd.ap()[st * TST:(st + 1) * TST, :]
                          .rearrange("(j p) d -> p j d", p=128),
                          xn_t[:]))
                for dk in range(NDK):
                    dstep(sync, "sp", s_xnt[fb], [xnd_key + "0"], [fkeys[dk]],
                          lambda dk=dk: sync.dma_start(
                              xn_f[fb][:, dk * TST:(dk + 1) * TST],
                              xnd.ap()[st * TST:(st + 1) * TST, dk * 128:(dk + 1) * 128],
                              transpose=True))

            for fk in fkeys:
                track[fk] = [(s_xnt[fb], s_xnt[fb].n), {}]
            track[xnd_key + "0"][1] = {s_xnt[fb]: s_xnt[fb].n}
            a_fk[st] = fkeys
        return finish

    par = {("R", mb): 0 for mb in range(NMB)}
    par.update({("I", mb): 0 for mb in range(NMB)})

    SL = [slice(m * TST, (m + 1) * TST) for m in range(NMB)]

    def emit_ew_a1(m):
        @block.vector
        def _(v, m=m):
            step(v, "dve", s_dve, [f"pkR{m}", f"pvR{m}"], [f"ta{m}", "rta"],
                 lambda: v.scalar_tensor_tensor(
                     ta[m][:], proj["kR"][:, SL[m]], 1.0, proj["vR"][:, SL[m]],
                     op0=OP.bypass, op1=OP.mult, accum_out=rred["ta"][:, m:m + 1]))
            step(v, "dve", s_dve, [f"pkI{m}", f"pvI{m}"], [f"tb{m}", "rtb"],
                 lambda: v.scalar_tensor_tensor(
                     tb[m][:], proj["kI"][:, SL[m]], 1.0, proj["vI"][:, SL[m]],
                     op0=OP.bypass, op1=OP.mult, accum_out=rred["tb"][:, m:m + 1]))
            step(v, "dve", s_dve, [f"pkR{m}", f"pvI{m}"], [f"tc{m}", "rtc"],
                 lambda: v.scalar_tensor_tensor(
                     tc[m][:], proj["kR"][:, SL[m]], 1.0, proj["vI"][:, SL[m]],
                     op0=OP.bypass, op1=OP.mult, accum_out=rred["tc"][:, m:m + 1]))
            step(v, "dve", s_dve, [f"pkI{m}", f"pvR{m}"], [f"td{m}", "rtd"],
                 lambda: v.scalar_tensor_tensor(
                     td[m][:], proj["kI"][:, SL[m]], 1.0, proj["vR"][:, SL[m]],
                     op0=OP.bypass, op1=OP.mult, accum_out=rred["td"][:, m:m + 1]))

    def emit_ew_a2(m, zp, z_t):
        @block.vector
        def _(v, m=m):
            step(v, "dve", s_dve, [f"pkR{m}", f"pvR{m}"], [f"ta{m}"],
                 lambda: v.scalar_tensor_tensor(ta[m][:], proj["kR"][:, SL[m]], 1.0,
                                                proj["vR"][:, SL[m]],
                                                op0=OP.bypass, op1=OP.mult))
            step(v, "dve", s_dve, [f"pkI{m}", f"pvI{m}"], [f"tb{m}"],
                 lambda: v.scalar_tensor_tensor(tb[m][:], proj["kI"][:, SL[m]], 1.0,
                                                proj["vI"][:, SL[m]],
                                                op0=OP.bypass, op1=OP.mult))
            step(v, "dve", s_dve, [f"ta{m}", f"tb{m}"], [f"kvR{m}", f"cumR{m}"],
                 lambda: v.scalar_tensor_tensor(kv_t["R"][:, SL[m]], ta[m][:], 1.0,
                                                tb[m][:],
                                                op0=OP.bypass, op1=OP.subtract))
            if m == 0:
                step(v, "dve", s_dve, ["ta0"], ["kvR0", "cumR0"],
                     lambda: v.tensor_copy(kv_t["R"][0:1, SL[0]], ta[0][0:1, :]))

        @block.gpsimd
        def _(g, m=m):
            step(g, "pool", s_pool, [f"pkR{m}", f"pvI{m}"], [f"tc{m}"],
                 lambda: g.tensor_mul(tc[m][:], proj["kR"][:, SL[m]], proj["vI"][:, SL[m]]))
            step(g, "pool", s_pool, [f"pkI{m}", f"pvR{m}"], [f"td{m}"],
                 lambda: g.tensor_mul(td[m][:], proj["kI"][:, SL[m]], proj["vR"][:, SL[m]]))
            step(g, "pool", s_pool, [f"tc{m}", f"td{m}"], [f"kvI{m}", f"cumI{m}"],
                 lambda: g.tensor_tensor(kv_t["I"][:, SL[m]], tc[m][:], td[m][:], op=OP.add))
            if m == 0:
                step(g, "pool", s_pool, ["tb0"], ["kvI0", "cumI0"],
                     lambda: g.tensor_copy(kv_t["I"][0:1, SL[0]], tb[0][0:1, :]))

        @block.vector
        def _(v, m=m):
            step(v, "dve", s_dve, [f"kvR{m}", "ones", "carryR"], [f"cumR{m}"],
                 lambda: v.tensor_tensor_scan(
                     cum_t["R"][:, SL[m]], ones_t[:], kv_t["R"][:, SL[m]],
                     carryR[:, m:m + 1], op0=OP.mult, op1=OP.add))
            step(v, "dve", s_dve, [f"kvI{m}", "ones", "carryI"], [f"cumI{m}"],
                 lambda: v.tensor_tensor_scan(
                     cum_t["I"][:, SL[m]], ones_t[:], kv_t["I"][:, SL[m]],
                     carryI[:, m:m + 1], op0=OP.mult, op1=OP.add))
            step(v, "dve", s_dve, [f"cumR{m}", f"pqR{m}"], [f"ta{m}"],
                 lambda: v.scalar_tensor_tensor(ta[m][:], cum_t["R"][:, SL[m]], 1.0,
                                                proj["qR"][:, SL[m]],
                                                op0=OP.bypass, op1=OP.mult))
            step(v, "dve", s_dve, [f"cumI{m}", f"pqI{m}"], [f"tb{m}"],
                 lambda: v.scalar_tensor_tensor(tb[m][:], cum_t["I"][:, SL[m]], 1.0,
                                                proj["qI"][:, SL[m]],
                                                op0=OP.bypass, op1=OP.mult))
            step(v, "dve", s_dve, [f"ta{m}", f"tb{m}"], [f"zR{m}_{zp}"],
                 lambda: v.scalar_tensor_tensor(z_t["R"][:, SL[m]], ta[m][:], 1.0,
                                                tb[m][:], op0=OP.bypass, op1=OP.add))
            if m == 0:
                step(v, "dve", s_dve, ["ta0"], [f"zR0_{zp}"],
                     lambda: v.tensor_copy(z_t["R"][0:1, SL[0]], ta[0][0:1, :]))

        @block.gpsimd
        def _(g, m=m):
            step(g, "pool", s_pool, [f"cumI{m}", f"pqR{m}"], [f"tc{m}"],
                 lambda: g.tensor_mul(tc[m][:], cum_t["I"][:, SL[m]],
                                      proj["qR"][:, SL[m]]))
            step(g, "pool", s_pool, [f"cumR{m}", f"pqI{m}"], [f"td{m}"],
                 lambda: g.tensor_mul(td[m][:], cum_t["R"][:, SL[m]],
                                      proj["qI"][:, SL[m]]))
            step(g, "pool", s_pool, [f"tc{m}", f"td{m}"], [f"zI{m}_{zp}"],
                 lambda: g.tensor_tensor(z_t["I"][:, SL[m]], tc[m][:], td[m][:],
                                         op=OP.subtract))
            if m == 0:
                step(g, "pool", s_pool, ["tb0"], [f"zI0_{zp}"],
                     lambda: g.tensor_copy(z_t["I"][0:1, SL[0]], tb[0][0:1, :]))

    def emit_projwaves(st, is_a1, fkeys, mid_cb=None):
        fb = st % 2
        zp = st % 2
        z_t = z_t2[zp]
        plist = ("kR", "kI", "vR", "vI") if is_a1 else ("kR", "kI", "vR", "vI", "qR", "qI")
        P = len(plist)
        chains = [(pn, mb) for mb in range(NMB) for pn in plist]
        mid_group = 1 if is_a1 else 4
        ew_done = 0
        for gi in range(0, len(chains), 4):
            group = chains[gi:gi + 4]
            pend = []
            for pn, mb in group:
                reg = singles[sidx[0] % 4]
                sidx[0] += 1

                @block.tensor
                def _(te, pn=pn, mb=mb, reg=reg, fb=fb, fkeys=fkeys):
                    dep_wait(te, "pe", fkeys + ["A" + pn, "b" + pn, "ones1"], [])
                    pacq(te, reg)
                    for dk in range(NDK):
                        te.matmul(
                            reg.ap,
                            sA[pn][:, dk * NB + mb * 128: dk * NB + (mb + 1) * 128],
                            xn_f[fb][:, dk * TST:(dk + 1) * TST],
                            start=(dk == 0), stop=False)
                    inst = te.matmul(reg.ap,
                                     sbias[pn][0:1, mb * 128:(mb + 1) * 128],
                                     ones1_t[0:1, :],
                                     start=False, stop=True)
                    v = s_pe.inc(inst)
                    record(s_pe, v, fkeys + ["A" + pn, "b" + pn, "ones1"], [])
                pend.append((pn, mb, reg, s_pe.n))

            @block.scalar
            def _(sc, pend=pend):
                for pn, mb, reg, pe_done in pend:
                    wge(sc, "act", s_pe, pe_done)
                    step(sc, "act", s_act, [], [f"p{pn}{mb}"],
                         lambda pn=pn, mb=mb, reg=reg:
                         sc.copy(proj[pn][:, mb * TST:(mb + 1) * TST], reg.ap))
                    prel(reg, s_act, s_act.n)

            while ew_done < NMB and ew_done * P + P - 1 < gi + 4:
                if is_a1:
                    emit_ew_a1(ew_done)
                else:
                    emit_ew_a2(ew_done, zp, z_t)
                ew_done += 1
            if gi // 4 == mid_group and mid_cb is not None:
                mid_cb()
        assert ew_done == NMB

        if is_a1:
            @block.vector
            def _(v):
                ppR = par[("R", 0)]
                ppI = par[("I", 0)]
                step(v, "dve", s_dve, ["rta", "rtb"], ["racc0"],
                     lambda: v.tensor_sub(racc_t[:, 0:NMB], rred["ta"][:], rred["tb"][:]))
                step(v, "dve", s_dve, ["rta", "racc0"], ["racc0"],
                     lambda: v.tensor_copy(racc_t[0:1, 0:1], rred["ta"][0:1, 0:1]))
                step(v, "dve", s_dve, ["racc0", f"accR{ppR}"], [f"accR{1 - ppR}"],
                     lambda ppR=ppR: v.tensor_add(acc[("R", 1 - ppR)][:], acc[("R", ppR)][:],
                                                  racc_t[:, 0:NMB]))
                par[("R", 0)] = 1 - ppR
                step(v, "dve", s_dve, ["rtc", "rtd"], ["racc1"],
                     lambda: v.tensor_add(racc_t[:, NMB:2 * NMB], rred["tc"][:], rred["td"][:]))
                step(v, "dve", s_dve, ["rtb", "racc1"], ["racc1"],
                     lambda: v.tensor_copy(racc_t[0:1, NMB:NMB + 1], rred["tb"][0:1, 0:1]))
                step(v, "dve", s_dve, ["racc1", f"accI{ppI}"], [f"accI{1 - ppI}"],
                     lambda ppI=ppI: v.tensor_add(acc[("I", 1 - ppI)][:], acc[("I", ppI)][:],
                                                  racc_t[:, NMB:2 * NMB]))
                par[("I", 0)] = 1 - ppI
            return

        @block.vector
        def _(v):
            step(v, "dve", s_dve, [f"cumR{m}" for m in range(NMB)], ["carryR"],
                 lambda: v.tensor_copy(
                     carryR[:], bass.AP(cum_t["R"], TST - 1, [[NMB * TST, 128], [TST, NMB]])))
            step(v, "dve", s_dve, [f"cumI{m}" for m in range(NMB)], ["carryI"],
                 lambda: v.tensor_copy(
                     carryI[:], bass.AP(cum_t["I"], TST - 1, [[NMB * TST, 128], [TST, NMB]])))

    def emit_tail(st):
        zp = st % 2
        z_t = z_t2[zp]
        zkeys = [f"z{RI}{m}_{zp}" for RI in "RI" for m in range(NMB)]
        for tci in range(NTC):
            reg = doubles[didx[0] % 2]
            didx[0] += 1

            @block.tensor
            def _(te, tci=tci, reg=reg):
                dep_wait(te, "pe", zkeys + ["sG"], [])
                pacq(te, reg)
                for gk in range(8):
                    zt = z_t["R"] if gk < 4 else z_t["I"]
                    mbk = gk % 4
                    lhsT = zt[:, mbk * TST + tci * 128: mbk * TST + (tci + 1) * 128]
                    for h in range(2):
                        inst = te.matmul(reg.ap[:, h * 512:(h + 1) * 512], lhsT,
                                         sG[:, gk * D + h * 512: gk * D + (h + 1) * 512],
                                         start=(gk == 0), stop=(gk == 7))
                v = s_pe.inc(inst)
                record(s_pe, v, zkeys + ["sG"], [])
            ifft_done = s_pe.n
            ab = tci % 2

            @block.scalar
            def _(sc, reg=reg, ab=ab, tci=tci, ifft_done=ifft_done):
                wge(sc, "act", s_pe, ifft_done)
                step(sc, "act", s_act, [], [f"attn{ab}"],
                     lambda: sc.copy(attn_t[ab][:], reg.ap))
                step(sc, "act", s_act, [], [f"attn{1 - ab}", "S"],
                     lambda: sc.activation(attn_t[1 - ab][:], reg.ap, AF.Square,
                                           accum_out=S_t[:, tci:tci + 1]))
                step(sc, "act", s_act, ["S"], ["lnS"],
                     lambda: sc.activation(lnS_t[:, tci:tci + 1], S_t[:, tci:tci + 1], AF.Ln))
                step(sc, "act", s_act, ["lnS", "cst2"], ["sqS"],
                     lambda: sc.activation(sqS_t[:, tci:tci + 1], lnS_t[:, tci:tci + 1],
                                           AF.Exp, scale=cst_t[:, 2:3]))
            prel(reg, s_act, s_act.n - 2)

            pb = tci % 2

            @block.vector
            def _(v, tci=tci, ab=ab, pb=pb):
                step(v, "dve", s_dve, ["sqS"], ["inv1"],
                     lambda: v.tensor_scalar(inv_t[:, tci:tci + 1], sqS_t[:, tci:tci + 1],
                                             1024.0, 1e-9, op0=OP.mult, op1=OP.add))
                step(v, "dve", s_dve, ["inv1"], ["inv"],
                     lambda: v.reciprocal(inv_t[:, tci:tci + 1], inv_t[:, tci:tci + 1]))
                step(v, "dve", s_dve, [f"attn{ab}", "inv"], [f"part{pb}"],
                     lambda: v.tensor_scalar(part_t[pb][:], attn_t[ab][:],
                                             inv_t[:, tci:tci + 1], None, op0=OP.mult))

            @block.sync
            def _(sync, st=st, tci=tci, pb=pb):
                dstep(sync, "sp", s_pw[pb], [f"part{pb}"], [f"partial_dram{tci % 4}"],
                      lambda: sync.dma_start(
                          partial_dram.ap()[st * TST + tci * 128: st * TST + (tci + 1) * 128, :],
                          part_t[pb][:]))

    xload_sched = ([(st, x_extra) for st in range(NST)]
                   + [(st, x_own) for st in range(NST)])
    xl_i = [0]

    def emit_next_xload():
        if xl_i[0] < len(xload_sched):
            st, xd = xload_sched[xl_i[0]]
            xl_i[0] += 1
            emit_xload(st, xd)

    a_fk = {}
    emit_next_xload()
    emit_next_xload()
    for it in range(NST + 1):
        mid = None
        if it < NST:
            emit_prep_pre(it)
            mid = mk_prep_finish(it, None, None)
        else:
            emit_prep_pre(0)
            mid = mk_prep_finish(0, xn_dram, "xn_dram")
        if it >= 1:
            emit_projwaves(it - 1, True, a_fk[it - 1], mid)
        elif mid is not None:
            mid()
        emit_next_xload()
        emit_conv_tasks(4)

    @block.vector
    def _(v):
        ppR, ppI = par[("R", 0)], par[("I", 0)]
        step(v, "dve", s_dve, [f"accR{ppR}", "scl3"], ["carryR"],
             lambda ppR=ppR: v.tensor_scalar(
                 carryR[:], acc[("R", ppR)][:], s_scales[:, 3:4], None, op0=OP.mult))
        step(v, "dve", s_dve, [f"accI{ppI}", "scl3"], ["carryI"],
             lambda ppI=ppI: v.tensor_scalar(
                 carryI[:], acc[("I", ppI)][:], s_scales[:, 3:4], None, op0=OP.mult))

    for it in range(NST + 2):
        mid = None
        if it == 0:
            emit_prep_pre(1)
            mk_prep_finish(1, xn_dram, "xn_dram")()
        if 2 <= it < NST:
            emit_prep_pre(it)
            mid = mk_prep_finish(it, xn_dram, "xn_dram")
        if 1 <= it <= NST:
            emit_projwaves(it - 1, False, a_fk[it - 1], mid)
        if 1 <= it < NST:
            emit_next_xload()
        if it >= 2:
            emit_tail(it - 2)
        emit_conv_tasks(4)

    emit_conv_tasks(len(conv_tasks))
    assert conv_i[0] == len(conv_tasks)

    es_a.close()

    es_b = ExitStack()
    K0d_t = mk_sb(es_b, "K0d_t", [128, NDK * F0], BF16)
    r0_t = mk_sb(es_b, "r0_t", [128, NF0 * TST], BF16)
    r1_t = mk_sb(es_b, "r1_t", [128, NF1 * TST], BF16)
    w1ring = [mk_sb(es_b, f"w1r{i}", [128, F0], BF16) for i in range(3)]
    w2ring = mk_sb(es_b, "w2ring", [128, 4 * D], BF16)

    part_t.append(mk_sb(es_b, "part_t2", [128, D], BF16))
    part_t.append(mk_sb(es_b, "part_t3", [128, D], BF16))
    bx_t = [mk_sb(es_b, f"bx_t{i}", [128, D], F32) for i in range(4)]

    end_marks = [(s_pe, s_pe.n), (s_act, s_act.n), (s_dve, s_dve.n),
                 (s_pool, s_pool.n),
                 (s_xnw, s_xnw.n), (s_xnw2, s_xnw2.n), (s_wld, s_wld.n),
                 (s_wldb, s_wldb.n), (s_bld, s_bld.n)]
    end_marks += [(c, c.n) for c in s_x + s_cv + s_xnt + s_pw + s_cw]

    def barrier(eng, key):
        for c, v in end_marks:
            wge(eng, key, c, v)

    @block.sync
    def _(sync):
        barrier(sync, "sp")
        dstep(sync, "sp", s_k0d, ["k0s"], ["K0da"],
              lambda: sync.dma_start(
                  K0d_t[:, 0:3 * F0],
                  k0s.ap()[0:3 * 128, :].rearrange("(k p) m -> p k m", p=128)))

    @block.scalar
    def _(sc):
        barrier(sc, "act")
        dstep(sc, "act", s_xnw2, ["k0s"], ["K0db"],
              lambda: sc.dma_start(
                  K0d_t[:, 3 * F0:6 * F0],
                  k0s.ap()[3 * 128:6 * 128, :].rearrange("(k p) m -> p k m", p=128)))

    @block.gpsimd
    def _(g):
        barrier(g, "pool")
        dstep(g, "pool", s_bld, ["k0s"], ["K0dc"],
              lambda: g.dma_start(
                  K0d_t[:, 6 * F0:8 * F0],
                  k0s.ap()[6 * 128:8 * 128, :].rearrange("(k p) m -> p k m", p=128)))



    def emit_prep_b(j):
        fb = j % 2
        fkeys = [f"xnf{fb}_{dk}" for dk in range(NDK)]

        @block.sync
        def _(sync, j=j, fb=fb, fkeys=fkeys):
            for dk in range(NDK):
                dstep(sync, "sp", s_xnt[fb], ["xn_dram0"], [fkeys[dk]],
                      lambda dk=dk: sync.dma_start(
                          xn_f[fb][:, dk * TST:(dk + 1) * TST],
                          xn_dram.ap()[j * TST:(j + 1) * TST, dk * 128:(dk + 1) * 128],
                          transpose=True))
            for fk in fkeys:
                track[fk] = [(s_xnt[fb], s_xnt[fb].n), {}]
        return fkeys

    def emit_compute_b(j, fkeys):
        fb = j % 2
        for tci in range(NTC):
            @block.gpsimd
            def _(g, j=j, tci=tci):
                dstep(g, "pool", s_pr[tci], [f"partial_dram{tci}"], [f"bpart{tci}"],
                      lambda: g.dma_start(
                          part_t[tci][:],
                          partial_dram.ap()[j * TST + tci * 128: j * TST + (tci + 1) * 128, :]))
                dstep(g, "pool", s_bx[tci], [], [f"bx{tci}"],
                      lambda: g.dma_start(
                          bx_t[tci][:],
                          x_own.ap()[j * TST + tci * 128: j * TST + (tci + 1) * 128, :]))

        for g0 in range(0, NF0, 4):
            pend = []
            for m in range(g0, g0 + 4):
                reg = singles[sidx[0] % 4]
                sidx[0] += 1

                @block.tensor
                def _(te, m=m, reg=reg, fb=fb, fkeys=fkeys):
                    dep_wait(te, "pe", fkeys + ["K0da", "K0db", "K0dc"], [])
                    pacq(te, reg)
                    for dk in range(NDK):
                        inst = te.matmul(
                            reg.ap,
                            K0d_t[:, dk * F0 + m * 128: dk * F0 + (m + 1) * 128],
                            xn_f[fb][:, dk * TST:(dk + 1) * TST],
                            start=(dk == 0), stop=(dk == NDK - 1))
                    v = s_pe.inc(inst)
                    record(s_pe, v, fkeys + ["K0da", "K0db", "K0dc"], [])
                pend.append((m, reg, s_pe.n))

            @block.scalar
            def _(sc, pend=pend):
                for m, reg, pe_done in pend:
                    wge(sc, "act", s_pe, pe_done)
                    step(sc, "act", s_act, ["scl0"], [f"r0_{m}"],
                         lambda m=m, reg=reg: sc.activation(
                             r0_t[:, m * TST:(m + 1) * TST], reg.ap,
                             AF.Relu, scale=s_scales[:, 0:1]))
                    prel(reg, s_act, s_act.n)
        r0keys = [f"r0_{m}" for m in range(NF0)]

        for c in range(3):
            @block.gpsimd
            def _(g, c=c):
                wge(g, "pool", s_cw[0], s_cw[0].n)
                wge(g, "pool", s_cw[1], s_cw[1].n)
                dstep(g, "pool", s_w1[c % 3], ["w1s"], [f"w1r{c % 3}"],
                      lambda: g.dma_start(w1ring[c % 3][:], w1s.ap()[c]))

            @block.vector
            def _(v, c=c):
                step(v, "dve", s_dve, [f"w1r{c % 3}"], [f"w1r{c % 3}"],
                     lambda: v.tensor_scalar(w1ring[c % 3][:], w1ring[c % 3][:],
                                             2.0, -1.0, op0=OP.mult, op1=OP.add))
        for g0 in range(0, NF1, 4):
            pend = []
            for c in range(g0, g0 + 4):
                reg = singles[sidx[0] % 4]
                sidx[0] += 1

                @block.tensor
                def _(te, c=c, reg=reg):
                    dep_wait(te, "pe", [f"w1r{c % 3}"] + r0keys, [])
                    pacq(te, reg)
                    for k in range(NF0):
                        inst = te.matmul(
                            reg.ap,
                            w1ring[c % 3][:, k * 128:(k + 1) * 128],
                            r0_t[:, k * TST:(k + 1) * TST],
                            start=(k == 0), stop=(k == NF0 - 1))
                    v = s_pe.inc(inst)
                    record(s_pe, v, [f"w1r{c % 3}"] + r0keys, [])
                pend.append((c, reg, s_pe.n))

                if c + 3 < NF1:
                    @block.gpsimd
                    def _(g, c=c):
                        dstep(g, "pool", s_w1[c % 3], ["w1s"], [f"w1r{c % 3}"],
                              lambda: g.dma_start(w1ring[c % 3][:], w1s.ap()[c + 3]))

                    @block.vector
                    def _(v, c=c):
                        step(v, "dve", s_dve, [f"w1r{c % 3}"], [f"w1r{c % 3}"],
                             lambda: v.tensor_scalar(w1ring[c % 3][:], w1ring[c % 3][:],
                                                     2.0, -1.0, op0=OP.mult, op1=OP.add))

            @block.scalar
            def _(sc, pend=pend):
                for c, reg, pe_done in pend:
                    wge(sc, "act", s_pe, pe_done)
                    step(sc, "act", s_act, ["scl1"], [f"r1_{c}"],
                         lambda c=c, reg=reg: sc.activation(
                             r1_t[:, c * TST:(c + 1) * TST], reg.ap,
                             AF.Relu, scale=s_scales[:, 1:2]))
                    prel(reg, s_act, s_act.n)
        r1keys = [f"r1_{c}" for c in range(NF1)]

        for c in range(4):
            @block.gpsimd
            def _(g, c=c):
                dstep(g, "pool", s_w2[c % 4], [], [f"w2r{c % 4}"],
                      lambda: g.dma_start(w2ring[:, (c % 4) * D:(c % 4 + 1) * D],
                                          k2u.ap()[c * 128:(c + 1) * 128, :]))

            @block.vector
            def _(v, c=c):
                step(v, "dve", s_dve, [f"w2r{c % 4}"], [f"w2r{c % 4}"],
                     lambda: v.tensor_scalar(w2ring[:, (c % 4) * D:(c % 4 + 1) * D],
                                             w2ring[:, (c % 4) * D:(c % 4 + 1) * D],
                                             2.0, -1.0, op0=OP.mult, op1=OP.add))
        for c in range(NF1):
            @block.tensor
            def _(te, c=c):
                dep_wait(te, "pe", [f"w2r{c % 4}", f"r1_{c}"], [])
                if c == 0:
                    for reg in doubles:
                        pacq(te, reg)
                for tci in range(NTC):
                    reg = doubles[tci]
                    lhsT = r1_t[:, c * TST + tci * 128: c * TST + (tci + 1) * 128]
                    for h in range(2):
                        inst = te.matmul(
                            reg.ap[:, h * 512:(h + 1) * 512], lhsT,
                            w2ring[:, (c % 4) * D + h * 512:(c % 4) * D + (h + 1) * 512],
                            start=(c == 0), stop=(c == NF1 - 1))
                v = s_pe.inc(inst)
                record(s_pe, v, [f"w2r{c % 4}", f"r1_{c}"], [])
            if c + 4 < NF1:
                @block.gpsimd
                def _(g, c=c):
                    dstep(g, "pool", s_w2[c % 4], [], [f"w2r{c % 4}"],
                          lambda: g.dma_start(w2ring[:, (c % 4) * D:(c % 4 + 1) * D],
                                              k2u.ap()[(c + 4) * 128:(c + 5) * 128, :]))

                @block.vector
                def _(v, c=c):
                    step(v, "dve", s_dve, [f"w2r{c % 4}"], [f"w2r{c % 4}"],
                         lambda: v.tensor_scalar(w2ring[:, (c % 4) * D:(c % 4 + 1) * D],
                                                 w2ring[:, (c % 4) * D:(c % 4 + 1) * D],
                                                 2.0, -1.0, op0=OP.mult, op1=OP.add))
        mm2_done = s_pe.n

        for tci in (2, 3, 0, 1):
            reg = doubles[tci]

            @block.vector
            def _(v, tci=tci, reg=reg):
                wge(v, "dve", s_pe, mm2_done)
                step(v, "dve", s_dve, [f"bx{tci}", "scl2"], [f"bx{tci}"],
                     lambda: v.scalar_tensor_tensor(bx_t[tci][:], reg.ap,
                                                    s_scales[:, 2:3], bx_t[tci][:],
                                                    op0=OP.mult, op1=OP.add))
            prel(reg, s_dve, s_dve.n)
        for tci in range(NTC):
            @block.vector
            def _(v, tci=tci):
                step(v, "dve", s_dve, [f"bpart{tci}"], [f"bx{tci}"],
                     lambda: v.tensor_add(bx_t[tci][:], bx_t[tci][:], part_t[tci][:]))

            @block.scalar
            def _(sc, j=j, tci=tci):
                dstep(sc, "act", s_ow[tci], [f"bx{tci}"], [],
                      lambda: sc.dma_start(
                          out.ap()[j * TST + tci * 128: j * TST + (tci + 1) * 128, :],
                          bx_t[tci][:]))

    b_fk = {}
    for j in range(NST + 1):
        if j < NST:
            b_fk[j] = emit_prep_b(j)
        if j >= 1:
            emit_compute_b(j - 1, b_fk[j - 1])

    @block.sync
    def _(sync):
        for c in s_ow:
            sync.wait_ge(c.h, c.n)

    es_b.close()
    es.close()
    return nc


def kernel(L0_attn_x, L0_attn_Wk, L0_attn_bk, L0_attn_Wv, L0_attn_bv,
           L0_attn_Wq, L0_attn_bq, L0_ffn_k0, L0_ffn_k1, L0_ffn_k2,
           L0_ffn_s0, L0_ffn_s1, L0_ffn_s2):
    x = np.ascontiguousarray(np.asarray(L0_attn_x, dtype=np.float32))
    B, S, _ = x.shape
    cosm, sinm, G = host_consts()
    k0a = np.ascontiguousarray(np.asarray(L0_ffn_k0)).view(np.uint8)
    k1a = np.ascontiguousarray(np.asarray(L0_ffn_k1)).view(np.uint8)
    k2a = np.ascontiguousarray(np.asarray(L0_ffn_k2)).view(np.uint8)

    def rep(v):
        return np.ascontiguousarray(
            np.broadcast_to(np.asarray(v, np.float32).reshape(1, 1), (128, 1)))

    common = {
        "Wk": np.ascontiguousarray(np.asarray(L0_attn_Wk, np.float32)),
        "Wv": np.ascontiguousarray(np.asarray(L0_attn_Wv, np.float32)),
        "Wq": np.ascontiguousarray(np.asarray(L0_attn_Wq, np.float32)),
        "bk": np.ascontiguousarray(np.asarray(L0_attn_bk, np.float32)),
        "bv": np.ascontiguousarray(np.asarray(L0_attn_bv, np.float32)),
        "bq": np.ascontiguousarray(np.asarray(L0_attn_bq, np.float32)),
        "k0u": k0a, "k1u": k1a, "k2u": k2a,
        "s0r": rep(L0_ffn_s0), "s1r": rep(L0_ffn_s1), "s2r": rep(L0_ffn_s2),
        "cosm": cosm, "sinm": sinm, "Gm": G,
        "identm": np.ascontiguousarray(np.eye(128, dtype=np.float32).astype(BF)),
    }
    in_maps = []
    for c in range(8):
        b, h = c // 2, c % 2
        im = dict(common)
        im["x_own"] = np.ascontiguousarray(x[b, h * TC:(h + 1) * TC])
        im["x_extra"] = np.ascontiguousarray(x[b, 0:TC])
        im["maskr"] = np.ascontiguousarray(
            np.full((128, 1), 1.0 if h == 1 else 0.0, np.float32))
        in_maps.append(im)

    nc = build_nc()
    res = run_bass_kernel_spmd(nc, in_maps, core_ids=list(range(8)))
    outp = np.empty((B, S, D), np.float32)
    for c in range(8):
        b, h = c // 2, c % 2
        outp[b, h * TC:(h + 1) * TC] = res.results[c]["out"]
    return outp


# version 31 (final)
# speedup vs baseline: 1.0784x; 1.0784x over previous
import numpy as np
import ml_dtypes
from contextlib import ExitStack

import concourse.bass as bass
import concourse.mybir as mybir
from concourse.bass_utils import run_bass_kernel_spmd
from concourse.alu_op_type import AluOpType as OP

F32 = mybir.dt.float32
BF16 = mybir.dt.bfloat16
U8 = mybir.dt.uint8
AF = mybir.ActivationFunctionType
AX = mybir.AxisListType.X
BF = ml_dtypes.bfloat16

D = 1024
NB = 512
NMB = 4
NDK = 8
TST = 512
NTC = 4
NST = 8
TC = NST * TST
F0 = 4096
F1 = 4096
NF0 = F0 // 128
NF1 = F1 // 128
CONV_Q = 1024


def host_consts():
    d = np.arange(D)[:, None].astype(np.float64)
    m = np.arange(NB)[None, :].astype(np.float64)
    ang = 2 * np.pi * d * m / D
    cosm = np.cos(ang)
    sinm = -np.sin(ang)
    sinm[:, 0] = (-1.0) ** np.arange(D)
    j = np.arange(D)[None, :].astype(np.float64)
    mm = np.arange(NB)[:, None].astype(np.float64)
    angG = 2 * np.pi * j * mm / D
    c = 1.0 / (D * np.sqrt(D))
    GR = 2 * c * np.cos(angG)
    GR[0, :] = c
    GI = -2 * c * np.sin(angG)
    GI[0, :] = c * ((-1.0) ** np.arange(D))
    G = np.concatenate([GR, GI], axis=0)
    return (np.ascontiguousarray(cosm.astype(np.float32).astype(BF)),
            np.ascontiguousarray(sinm.astype(np.float32).astype(BF)),
            np.ascontiguousarray(G.astype(np.float32).astype(BF)))


def build_nc(dbg_mode=False):
    nc = bass.Bass("TRN2", target_bir_lowering=False)

    x_own = nc.dram_tensor("x_own", [TC, D], F32, kind="ExternalInput")
    x_extra = nc.dram_tensor("x_extra", [TC, D], F32, kind="ExternalInput")
    Wmat = {p: nc.dram_tensor("W" + p, [D, D], F32, kind="ExternalInput")
            for p in ("k", "v", "q")}
    bvec = {p: nc.dram_tensor("b" + p, [D], F32, kind="ExternalInput")
            for p in ("k", "v", "q")}
    k0u = nc.dram_tensor("k0u", [2 * D, F0], U8, kind="ExternalInput")
    k1u = nc.dram_tensor("k1u", [F0, F1], U8, kind="ExternalInput")
    k2u = nc.dram_tensor("k2u", [F1, D], U8, kind="ExternalInput")
    s0r = nc.dram_tensor("s0r", [128, 1], F32, kind="ExternalInput")
    s1r = nc.dram_tensor("s1r", [128, 1], F32, kind="ExternalInput")
    s2r = nc.dram_tensor("s2r", [128, 1], F32, kind="ExternalInput")
    maskr = nc.dram_tensor("maskr", [128, 1], F32, kind="ExternalInput")
    cosm_d = nc.dram_tensor("cosm", [D, NB], BF16, kind="ExternalInput")
    sinm_d = nc.dram_tensor("sinm", [D, NB], BF16, kind="ExternalInput")
    Gm = nc.dram_tensor("Gm", [D, D], BF16, kind="ExternalInput")
    identm = nc.dram_tensor("identm", [128, 128], BF16, kind="ExternalInput")

    ikind = "ExternalOutput" if dbg_mode else "Internal"
    xn_dram = nc.dram_tensor("xn_dram", [TC, D], BF16, kind=ikind)
    xne_dram = nc.dram_tensor("xne_dram", [TC, D], BF16, kind=ikind)
    partial_dram = nc.dram_tensor("partial_dram", [TC, D], BF16, kind=ikind)
    w1s = nc.dram_tensor("w1s", [NF1, 128, F0], U8, kind=ikind)
    k0s = nc.dram_tensor("k0s", [D, F0], BF16, kind=ikind)
    out = nc.dram_tensor("out", [TC, D], F32, kind="ExternalOutput")

    es = ExitStack()

    def mk_sb(stack, name, shape, dt):
        return stack.enter_context(nc.sbuf_tensor(name, shape, dt))

    class Ctr:
        __slots__ = ("h", "n", "name")

        def __init__(self, name):
            self.h = es.enter_context(nc.semaphore(name))
            self.n = 0
            self.name = name

        def inc(self, inst, k=1):
            inst.then_inc(self.h, k)
            self.n += k
            return self.n

    marks = {}

    def wge(eng, key, ctr, val):
        if val <= 0:
            return
        if marks.get((key, ctr.name), 0) >= val:
            return
        eng.wait_ge(ctr.h, val)
        marks[(key, ctr.name)] = val

    track = {}

    def dep_wait(eng, ekey, ins, outs):
        need = {}
        for t in ins:
            e = track.get(t)
            if e and e[0] is not None:
                c, v = e[0]
                need[c] = max(need.get(c, 0), v)
        for t in outs:
            e = track.get(t)
            if e:
                if e[0] is not None:
                    c, v = e[0]
                    need[c] = max(need.get(c, 0), v)
                for c, v in e[1].items():
                    need[c] = max(need.get(c, 0), v)
        for c, v in need.items():
            wge(eng, ekey, c, v)

    def record(ctr, val, ins, outs):
        for t in ins:
            e = track.setdefault(t, [None, {}])
            e[1][ctr] = max(e[1].get(ctr, 0), val)
        for t in outs:
            track[t] = [(ctr, val), {}]

    def step(eng, ekey, ctr, ins, outs, emit_fn):
        dep_wait(eng, ekey, ins, outs)
        inst = emit_fn()
        v = ctr.inc(inst)
        record(ctr, v, ins, outs)
        return v

    def dstep(eng, ekey, ctr, ins, outs, emit_fn):
        dep_wait(eng, ekey, ins, outs)
        inst = emit_fn()
        v = ctr.inc(inst, 16)
        record(ctr, v, ins, outs)
        return v

    s_x = [Ctr("s_x0"), Ctr("s_x1")]
    s_ld = Ctr("s_ld")
    s_bld = Ctr("s_bld")
    s_wld = Ctr("s_wld")
    s_wldb = Ctr("s_wldb")
    s_cv = [Ctr(f"s_cv{i}") for i in range(4)]
    s_xnw = Ctr("s_xnw")
    s_xnw2 = Ctr("s_xnw2")
    s_xnt = [Ctr("s_xnt0"), Ctr("s_xnt1")]
    s_cw = [Ctr("s_cw0"), Ctr("s_cw1")]
    s_k0d = Ctr("s_k0d")
    s_w1 = [Ctr("s_w1a"), Ctr("s_w1b"), Ctr("s_w1c")]
    s_w2 = [Ctr("s_w2a"), Ctr("s_w2b"), Ctr("s_w2c"), Ctr("s_w2d")]
    s_pw = [Ctr(f"s_pw{i}") for i in range(4)]
    s_bx = [Ctr(f"s_bx{i}") for i in range(4)]
    s_pr = [Ctr(f"s_pr{i}") for i in range(4)]
    s_ow = [Ctr(f"s_ow{i}") for i in range(4)]
    s_pe = Ctr("s_pe")
    s_act = Ctr("s_act")
    s_dve = Ctr("s_dve")
    s_pool = Ctr("s_pool")

    xn_f = [mk_sb(es, f"xn_f{i}", [128, NDK * TST], BF16) for i in range(2)]
    part_t = [mk_sb(es, f"part_t{i}", [128, D], BF16) for i in range(2)]
    s_scales = mk_sb(es, "s_scales", [128, 6], F32)
    cst_t = mk_sb(es, "cst_t", [128, 3], F32)

    pd = [es.enter_context(nc.psum_tensor(f"pd{i}", [128, 1024], F32))
          for i in range(4)]

    class Region:
        __slots__ = ("ap", "banks")

        def __init__(self, ap, banks):
            self.ap = ap
            self.banks = banks

    bank_rel = [None] * 8
    singles = [Region(pd[2][:, 0:512], [4]), Region(pd[2][:, 512:1024], [5]),
               Region(pd[3][:, 0:512], [6]), Region(pd[3][:, 512:1024], [7])]
    doubles = [Region(pd[0][:, :], [0, 1]), Region(pd[1][:, :], [2, 3]),
               Region(pd[2][:, :], [4, 5]), Region(pd[3][:, :], [6, 7])]

    def pacq(te, reg):
        for b in reg.banks:
            if bank_rel[b] is not None:
                c, v = bank_rel[b]
                wge(te, "pe", c, v)

    def prel(reg, ctr, val):
        for b in reg.banks:
            bank_rel[b] = (ctr, val)

    sidx = [0]
    didx = [0]

    block = es.enter_context(nc.Block())

    es_a = ExitStack()
    sA = {p + ri: mk_sb(es_a, "A" + p + ri, [128, NDK * NB], BF16)
          for p in ("k", "v", "q") for ri in ("R", "I")}
    sG = mk_sb(es_a, "sG", [128, NDK * D], BF16)
    sbias = {p + ri: mk_sb(es_a, "b" + p + ri, [1, NB], BF16)
             for p in ("k", "v", "q") for ri in ("R", "I")}
    ones1_t = mk_sb(es_a, "ones1_t", [1, TST], BF16)
    ident_t = mk_sb(es_a, "ident_t", [128, 128], BF16)
    carryR = mk_sb(es_a, "carryR", [128, NMB], F32)
    carryI = mk_sb(es_a, "carryI", [128, NMB], F32)
    acc = {("R", i): mk_sb(es_a, f"accR{i}", [128, NMB], F32) for i in range(2)}
    acc.update({("I", i): mk_sb(es_a, f"accI{i}", [128, NMB], F32) for i in range(2)})

    es_a0 = ExitStack()
    wtmp2 = [mk_sb(es_a0, f"wtmp{i}", [128, NDK * D], BF16) for i in range(2)]
    cosm_t = mk_sb(es_a0, "cosm_t", [128, NDK * NB], BF16)
    sinm_t = mk_sb(es_a0, "sinm_t", [128, NDK * NB], BF16)
    bcol = {p: mk_sb(es_a0, "bcol_" + p, [128, NDK], BF16) for p in ("k", "v", "q")}

    @block.sync
    def _(sync):
        dstep(sync, "sp", s_ld, [], ["sG"],
              lambda: sync.dma_start(sG[:], Gm.ap().rearrange("(k p) j -> p k j", p=128)))
        dstep(sync, "sp", s_ld, [], ["scl0"],
              lambda: sync.dma_start(s_scales[:, 0:1], s0r.ap()))
        dstep(sync, "sp", s_ld, [], ["scl1"],
              lambda: sync.dma_start(s_scales[:, 1:2], s1r.ap()))
        dstep(sync, "sp", s_ld, [], ["scl2"],
              lambda: sync.dma_start(s_scales[:, 2:3], s2r.ap()))
        dstep(sync, "sp", s_ld, [], ["scl3"],
              lambda: sync.dma_start(s_scales[:, 3:4], maskr.ap()))
        dstep(sync, "sp", s_ld, [], ["cosm"],
              lambda: sync.dma_start(cosm_t[:], cosm_d.ap().rearrange("(k p) m -> p k m", p=128)))
        dstep(sync, "sp", s_ld, [], ["sinm"],
              lambda: sync.dma_start(sinm_t[:], sinm_d.ap().rearrange("(k p) m -> p k m", p=128)))
        dstep(sync, "sp", s_ld, [], ["ident"],
              lambda: sync.dma_start(ident_t[:], identm.ap()))
        for kk in ("sG", "scl0", "scl1", "scl2", "scl3", "cosm", "sinm", "ident"):
            track[kk] = [(s_ld, s_ld.n), {}]

    @block.gpsimd
    def _(g):
        for p in ("k", "v", "q"):
            dstep(g, "pool", s_bld, [], ["bcol" + p],
                  lambda p=p: g.dma_start(bcol[p][:],
                                          bvec[p].ap().rearrange("(k p) -> p k", p=128),
                                          allow_slow_non_contiguous=True))
        for p in ("k", "v", "q"):
            track["bcol" + p] = [(s_bld, s_bld.n), {}]

    @block.vector
    def _(v):
        step(v, "dve", s_dve, ["scl0"], ["scl0"],
             lambda: v.tensor_scalar(s_scales[:, 0:1], s_scales[:, 0:1], 2.0, None, op0=OP.mult))
        step(v, "dve", s_dve, ["scl1"], ["scl1"],
             lambda: v.tensor_scalar(s_scales[:, 1:2], s_scales[:, 1:2], 2.0, None, op0=OP.mult))
        step(v, "dve", s_dve, ["scl2"], ["scl2"],
             lambda: v.tensor_scalar(s_scales[:, 2:3], s_scales[:, 2:3], 2.0, None, op0=OP.mult))
        step(v, "dve", s_dve, ["scl1"], ["scl4"],
             lambda: v.tensor_scalar(s_scales[:, 4:5], s_scales[:, 1:2], 2.0, None, op0=OP.mult))
        step(v, "dve", s_dve, ["scl2"], ["scl5"],
             lambda: v.tensor_scalar(s_scales[:, 5:6], s_scales[:, 2:3], 2.0, None, op0=OP.mult))
        for ri in "RI":
            for i in range(2):
                step(v, "dve", s_dve, [], [f"acc{ri}{i}"],
                     lambda ri=ri, i=i: v.memset(acc[(ri, i)][:], 0.0))
        step(v, "dve", s_dve, [], ["carryR"], lambda: v.memset(carryR[:], 0.0))
        step(v, "dve", s_dve, [], ["carryI"], lambda: v.memset(carryI[:], 0.0))
        step(v, "dve", s_dve, [], ["cst0"], lambda: v.memset(cst_t[:, 0:1], 1e-5))
        step(v, "dve", s_dve, [], ["cst1"], lambda: v.memset(cst_t[:, 1:2], -0.5))
        step(v, "dve", s_dve, [], ["cst2"], lambda: v.memset(cst_t[:, 2:3], 0.5))
        step(v, "dve", s_dve, [], ["ones1"], lambda: v.memset(ones1_t[:], 1.0))

    for pi, p in enumerate(("k", "v", "q")):
        wb = pi % 2
        wtmp = wtmp2[wb]
        wkey = f"wtmp{wb}"

        wsem = s_wld if wb == 0 else s_wldb

        @block.gpsimd
        def _(g, p=p, wtmp=wtmp, wkey=wkey, wsem=wsem):
            dstep(g, "pool", wsem, [], [wkey],
                  lambda: g.dma_start(wtmp[:], Wmat[p].ap().rearrange("(k p) d -> p k d", p=128)))

        for ri, trig_t, trig_key in (("R", cosm_t, "cosm"), ("I", sinm_t, "sinm")):
            for dm in range(NDK):
                reg = singles[sidx[0] % 4]
                sidx[0] += 1

                @block.tensor
                def _(te, p=p, dm=dm, reg=reg, trig_t=trig_t, trig_key=trig_key,
                      wtmp=wtmp, wkey=wkey):
                    dep_wait(te, "pe", [wkey, trig_key], [])
                    pacq(te, reg)
                    for ek in range(NDK):
                        inst = te.matmul(
                            reg.ap,
                            wtmp[:, ek * D + dm * 128: ek * D + (dm + 1) * 128],
                            trig_t[:, ek * NB:(ek + 1) * NB],
                            start=(ek == 0), stop=(ek == NDK - 1))
                    v = s_pe.inc(inst)
                    record(s_pe, v, [wkey, trig_key], [])
                pe_done = s_pe.n

                @block.scalar
                def _(sc, p=p, ri=ri, dm=dm, reg=reg, pe_done=pe_done):
                    wge(sc, "act", s_pe, pe_done)
                    step(sc, "act", s_act, [], ["A" + p + ri],
                         lambda: sc.copy(sA[p + ri][:, dm * NB:(dm + 1) * NB], reg.ap))
                prel(reg, s_act, s_act.n)

    for p in ("k", "v", "q"):
        for ri, trig_t, trig_key in (("R", cosm_t, "cosm"), ("I", sinm_t, "sinm")):
            for mb in range(NMB):
                reg = singles[sidx[0] % 4]
                sidx[0] += 1

                @block.tensor
                def _(te, p=p, mb=mb, reg=reg, trig_t=trig_t, trig_key=trig_key):
                    dep_wait(te, "pe", [trig_key, "bcol" + p], [])
                    pacq(te, reg)
                    for ek in range(NDK):
                        inst = te.matmul(
                            reg.ap[0:1, mb * 128:(mb + 1) * 128],
                            bcol[p][:, ek:ek + 1],
                            trig_t[:, ek * NB + mb * 128: ek * NB + (mb + 1) * 128],
                            start=(ek == 0), stop=(ek == NDK - 1))
                    v = s_pe.inc(inst)
                    record(s_pe, v, [trig_key, "bcol" + p], [])
                pe_done = s_pe.n

                @block.scalar
                def _(sc, p=p, ri=ri, mb=mb, reg=reg, pe_done=pe_done):
                    wge(sc, "act", s_pe, pe_done)
                    step(sc, "act", s_act, [], ["b" + p + ri],
                         lambda: sc.copy(sbias[p + ri][0:1, mb * 128:(mb + 1) * 128],
                                         reg.ap[0:1, mb * 128:(mb + 1) * 128]))
                prel(reg, s_act, s_act.n)

    a0_pe_done = s_pe.n
    a0_act_done = s_act.n
    es_a0.close()

    x_t = [mk_sb(es_a, f"x_t{i}", [128, NTC * D], BF16) for i in range(2)]
    xn_t = mk_sb(es_a, "xn_t", [128, NTC * D], BF16)
    stats_t = mk_sb(es_a, "stats_t", [128, 48], F32)
    mv_t = mk_sb(es_a, "mv_t", [128, 2 * NTC], F32)
    rstd_t = mk_sb(es_a, "rstd_t", [128, NTC], F32)
    negmr_t = mk_sb(es_a, "negmr_t", [128, NTC], F32)
    mean_t = mk_sb(es_a, "mean_t", [128, NTC], F32)
    var_t = mk_sb(es_a, "var_t", [128, NTC], F32)
    lnv_t = mk_sb(es_a, "lnv_t", [128, NTC], F32)
    proj = {p + ri: mk_sb(es_a, "p" + p + ri, [128, NMB * TST], BF16)
            for p in ("k", "v", "q") for ri in ("R", "I")}
    ta = [mk_sb(es_a, f"ta{m}", [128, TST], BF16) for m in range(NMB)]
    tb = [mk_sb(es_a, f"tb{m}", [128, TST], BF16) for m in range(NMB)]
    tc = [mk_sb(es_a, f"tc{m}", [128, TST], BF16) for m in range(NMB)]
    td = [mk_sb(es_a, f"td{m}", [128, TST], BF16) for m in range(NMB)]
    kv_t = {"R": mk_sb(es_a, "kvR_t", [128, NMB * TST], BF16),
            "I": mk_sb(es_a, "kvI_t", [128, NMB * TST], BF16)}
    cum_t = kv_t
    z_t2 = [{"R": mk_sb(es_a, f"zR_t{i}", [128, NMB * TST], BF16),
             "I": mk_sb(es_a, f"zI_t{i}", [128, NMB * TST], BF16)} for i in range(2)]
    attn_t = [mk_sb(es_a, f"attn_t{i}", [128, D], BF16) for i in range(2)]
    rred = {nm: mk_sb(es_a, "r" + nm, [128, NMB], F32)
            for nm in ("ta", "tb", "tc", "td")}
    racc_t = mk_sb(es_a, "racc_t", [128, 2 * NMB], F32)
    S_t = mk_sb(es_a, "S_t", [128, NTC], F32)
    lnS_t = mk_sb(es_a, "lnS_t", [128, NTC + 1], F32)
    sqS_t = mk_sb(es_a, "sqS_t", [128, NTC], F32)
    inv_t = mk_sb(es_a, "inv_t", [128, NTC], F32)
    ones_t = mk_sb(es_a, "ones_t", [128, TST], BF16)
    conv_t = [mk_sb(es_a, f"conv_t{i}", [128, CONV_Q], BF16) for i in range(4)]
    conv_o = [mk_sb(es_a, f"conv_o{i}", [128, CONV_Q], BF16) for i in range(2)]

    @block.vector
    def _(v):
        step(v, "dve", s_dve, [], ["ones"], lambda: v.memset(ones_t[:], 1.0))

    conv_tasks = ([("k0", i) for i in range(4 * NDK)]
                  + [("w1", k) for k in range(NF0)])
    conv_i = [0]

    def emit_conv_tasks(n):
        for _ in range(n):
            if conv_i[0] >= len(conv_tasks):
                return
            kind, idx = conv_tasks[conv_i[0]]
            conv_i[0] += 1
            cb = conv_i[0] % 2
            c4 = (2 * conv_i[0]) % 4
            if kind == "w1":
                @block.sync
                def _(sync, k=idx):
                    dstep(sync, "sp", s_cw[k % 2], [], ["w1s"],
                          lambda: sync.dma_start(
                              w1s.ap()[:, :, k * 128:(k + 1) * 128]
                              .rearrange("c p m -> p c m"),
                              k1u.ap()[k * 128:(k + 1) * 128, :]))
            else:
                kd, hh = idx // 4, idx % 4

                @block.gpsimd
                def _(g, kd=kd, hh=hh, c4=c4):
                    dstep(g, "pool", s_cv[c4], [], [f"cv{c4}"],
                          lambda: g.dma_start(
                              conv_t[c4][:],
                              k0u.ap()[kd * 128:(kd + 1) * 128,
                                       hh * CONV_Q:(hh + 1) * CONV_Q]))
                    dstep(g, "pool", s_cv[c4 + 1], [], [f"cv{c4 + 1}"],
                          lambda: g.dma_start(
                              conv_t[c4 + 1][:],
                              k0u.ap()[D + kd * 128: D + (kd + 1) * 128,
                                       hh * CONV_Q:(hh + 1) * CONV_Q]))

                @block.gpsimd
                def _(g, cb=cb, c4=c4):
                    step(g, "pool", s_pool, [f"cv{c4}", f"cv{c4 + 1}"], [f"cvo{cb}"],
                         lambda: g.tensor_tensor(conv_o[cb][:], conv_t[c4][:],
                                                 conv_t[c4 + 1][:], op=OP.subtract))

                @block.sync
                def _(sync, kd=kd, hh=hh, cb=cb):
                    dstep(sync, "sp", s_cw[cb], [f"cvo{cb}"], ["k0s"],
                          lambda: sync.dma_start(
                              k0s.ap()[kd * 128:(kd + 1) * 128,
                                       hh * CONV_Q:(hh + 1) * CONV_Q],
                              conv_o[cb][:]))

    def emit_xload(st, xd):
        xb = st % 2
        xk = f"x{xb}"

        @block.gpsimd
        def _(g, st=st, xb=xb, xk=xk):
            wge(g, "pool", s_pe, a0_pe_done)
            wge(g, "pool", s_act, a0_act_done)
            dstep(g, "pool", s_x[xb], [], [xk],
                  lambda: g.dma_start(
                      x_t[xb][:],
                      xd.ap()[st * TST:(st + 1) * TST, :].rearrange("(j p) d -> p j d", p=128)))

    def emit_prep_pre(st):
        xb = st % 2
        xk = f"x{xb}"

        @block.vector
        def _(v, xb=xb, xk=xk):
            for jj in range(2 * NTC):
                step(v, "dve", s_dve, [xk], [f"stats{jj}"],
                     lambda jj=jj: v.bn_stats(stats_t[:, jj * 6:(jj + 1) * 6],
                                              x_t[xb][:, jj * 512:(jj + 1) * 512]))
            for j in range(NTC):
                step(v, "dve", s_dve, [f"stats{2 * j}", f"stats{2 * j + 1}"], ["mv"],
                     lambda j=j: v.bn_aggr(mv_t[:, 2 * j:2 * j + 2],
                                           stats_t[:, 12 * j:12 * (j + 1)]))
            step(v, "dve", s_dve, ["mv"], ["meanv"],
                 lambda: v.tensor_copy(mean_t[:], bass.AP(mv_t, 0, [[2 * NTC, 128], [2, NTC]])))
            step(v, "dve", s_dve, ["mv"], ["varv"],
                 lambda: v.tensor_copy(var_t[:], bass.AP(mv_t, 1, [[2 * NTC, 128], [2, NTC]])))

    def mk_prep_finish(st, xnd, xnd_key):
        def finish(st=st, xnd=xnd, xnd_key=xnd_key):
            xb = st % 2
            xk = f"x{xb}"

            @block.scalar
            def _(sc):
                step(sc, "act", s_act, ["varv", "cst0"], ["lnx"],
                     lambda: sc.activation(lnv_t[:], var_t[:], AF.Ln, bias=cst_t[:, 0:1]))
                step(sc, "act", s_act, ["lnx", "cst1"], ["rstd"],
                     lambda: sc.activation(rstd_t[:], lnv_t[:], AF.Exp, scale=cst_t[:, 1:2]))

            @block.vector
            def _(v, xb=xb, xk=xk):
                step(v, "dve", s_dve, ["meanv", "rstd"], ["negmr"],
                     lambda: v.scalar_tensor_tensor(negmr_t[:], mean_t[:], -1.0, rstd_t[:],
                                                    op0=OP.mult, op1=OP.mult))
                for jj in range(NTC):
                    step(v, "dve", s_dve, [xk, "negmr", "rstd"], [f"xn{jj}"],
                         lambda jj=jj: v.tensor_scalar(xn_t[:, jj * D:(jj + 1) * D],
                                                       x_t[xb][:, jj * D:(jj + 1) * D],
                                                       rstd_t[:, jj:jj + 1],
                                                       negmr_t[:, jj:jj + 1],
                                                       op0=OP.mult, op1=OP.add))

            fb = st % 2
            fkeys = [f"xnf{fb}_{dk}" for dk in range(NDK)]
            xnk = [f"xn{jj}" for jj in range(NTC)]

            if xnd is None:
                for d0 in range(0, NDK, 4):
                    pend = []
                    for dk in range(d0, d0 + 4):
                        reg = singles[sidx[0] % 4]
                        sidx[0] += 1

                        @block.tensor
                        def _(te, dk=dk, reg=reg):
                            dep_wait(te, "pe", xnk + ["ident"], [])
                            pacq(te, reg)
                            for c in range(NTC):
                                inst = te.matmul(
                                    reg.ap[:, c * 128:(c + 1) * 128],
                                    xn_t[:, c * D + dk * 128: c * D + (dk + 1) * 128],
                                    ident_t[:],
                                    start=True, stop=True)
                            v = s_pe.inc(inst)
                            record(s_pe, v, xnk + ["ident"], [])
                        pend.append((dk, reg, s_pe.n))

                    @block.scalar
                    def _(sc, pend=pend, fb=fb, fkeys=fkeys):
                        for dk, reg, pe_done in pend:
                            wge(sc, "act", s_pe, pe_done)
                            step(sc, "act", s_act, [], [fkeys[dk]],
                                 lambda dk=dk, reg=reg:
                                 sc.copy(xn_f[fb][:, dk * TST:(dk + 1) * TST], reg.ap))
                            prel(reg, s_act, s_act.n)
                a_fk[st] = fkeys
                return

            @block.sync
            def _(sync, st=st, fb=fb, fkeys=fkeys):
                dstep(sync, "sp", s_xnw, xnk, [xnd_key + "0"],
                      lambda: sync.dma_start(
                          xnd.ap()[st * TST:(st + 1) * TST, :]
                          .rearrange("(j p) d -> p j d", p=128),
                          xn_t[:]))
                for dk in range(NDK):
                    dstep(sync, "sp", s_xnt[fb], [xnd_key + "0"], [fkeys[dk]],
                          lambda dk=dk: sync.dma_start(
                              xn_f[fb][:, dk * TST:(dk + 1) * TST],
                              xnd.ap()[st * TST:(st + 1) * TST, dk * 128:(dk + 1) * 128],
                              transpose=True))

            for fk in fkeys:
                track[fk] = [(s_xnt[fb], s_xnt[fb].n), {}]
            track[xnd_key + "0"][1] = {s_xnt[fb]: s_xnt[fb].n}
            a_fk[st] = fkeys
        return finish

    par = {("R", mb): 0 for mb in range(NMB)}
    par.update({("I", mb): 0 for mb in range(NMB)})

    SL = [slice(m * TST, (m + 1) * TST) for m in range(NMB)]

    def emit_ew_a1(m):
        @block.vector
        def _(v, m=m):
            step(v, "dve", s_dve, [f"pkR{m}", f"pvR{m}"], [f"ta{m}", "rta"],
                 lambda: v.scalar_tensor_tensor(
                     ta[m][:], proj["kR"][:, SL[m]], 1.0, proj["vR"][:, SL[m]],
                     op0=OP.bypass, op1=OP.mult, accum_out=rred["ta"][:, m:m + 1]))
            step(v, "dve", s_dve, [f"pkI{m}", f"pvI{m}"], [f"tb{m}", "rtb"],
                 lambda: v.scalar_tensor_tensor(
                     tb[m][:], proj["kI"][:, SL[m]], 1.0, proj["vI"][:, SL[m]],
                     op0=OP.bypass, op1=OP.mult, accum_out=rred["tb"][:, m:m + 1]))
            step(v, "dve", s_dve, [f"pkR{m}", f"pvI{m}"], [f"tc{m}", "rtc"],
                 lambda: v.scalar_tensor_tensor(
                     tc[m][:], proj["kR"][:, SL[m]], 1.0, proj["vI"][:, SL[m]],
                     op0=OP.bypass, op1=OP.mult, accum_out=rred["tc"][:, m:m + 1]))
            step(v, "dve", s_dve, [f"pkI{m}", f"pvR{m}"], [f"td{m}", "rtd"],
                 lambda: v.scalar_tensor_tensor(
                     td[m][:], proj["kI"][:, SL[m]], 1.0, proj["vR"][:, SL[m]],
                     op0=OP.bypass, op1=OP.mult, accum_out=rred["td"][:, m:m + 1]))

    def emit_ew_a2(m, zp, z_t):
        @block.vector
        def _(v, m=m):
            step(v, "dve", s_dve, [f"pkR{m}", f"pvR{m}"], [f"ta{m}"],
                 lambda: v.scalar_tensor_tensor(ta[m][:], proj["kR"][:, SL[m]], 1.0,
                                                proj["vR"][:, SL[m]],
                                                op0=OP.bypass, op1=OP.mult))
            step(v, "dve", s_dve, [f"pkI{m}", f"pvI{m}"], [f"tb{m}"],
                 lambda: v.scalar_tensor_tensor(tb[m][:], proj["kI"][:, SL[m]], 1.0,
                                                proj["vI"][:, SL[m]],
                                                op0=OP.bypass, op1=OP.mult))
            step(v, "dve", s_dve, [f"ta{m}", f"tb{m}"], [f"kvR{m}", f"cumR{m}"],
                 lambda: v.scalar_tensor_tensor(kv_t["R"][:, SL[m]], ta[m][:], 1.0,
                                                tb[m][:],
                                                op0=OP.bypass, op1=OP.subtract))
            if m == 0:
                step(v, "dve", s_dve, ["ta0"], ["kvR0", "cumR0"],
                     lambda: v.tensor_copy(kv_t["R"][0:1, SL[0]], ta[0][0:1, :]))

        @block.gpsimd
        def _(g, m=m):
            step(g, "pool", s_pool, [f"pkR{m}", f"pvI{m}"], [f"tc{m}"],
                 lambda: g.tensor_mul(tc[m][:], proj["kR"][:, SL[m]], proj["vI"][:, SL[m]]))
            step(g, "pool", s_pool, [f"pkI{m}", f"pvR{m}"], [f"td{m}"],
                 lambda: g.tensor_mul(td[m][:], proj["kI"][:, SL[m]], proj["vR"][:, SL[m]]))
            step(g, "pool", s_pool, [f"tc{m}", f"td{m}"], [f"kvI{m}", f"cumI{m}"],
                 lambda: g.tensor_tensor(kv_t["I"][:, SL[m]], tc[m][:], td[m][:], op=OP.add))
            if m == 0:
                step(g, "pool", s_pool, ["tb0"], ["kvI0", "cumI0"],
                     lambda: g.tensor_copy(kv_t["I"][0:1, SL[0]], tb[0][0:1, :]))

        @block.vector
        def _(v, m=m):
            step(v, "dve", s_dve, [f"kvR{m}", "ones", "carryR"], [f"cumR{m}"],
                 lambda: v.tensor_tensor_scan(
                     cum_t["R"][:, SL[m]], ones_t[:], kv_t["R"][:, SL[m]],
                     carryR[:, m:m + 1], op0=OP.mult, op1=OP.add))
            step(v, "dve", s_dve, [f"kvI{m}", "ones", "carryI"], [f"cumI{m}"],
                 lambda: v.tensor_tensor_scan(
                     cum_t["I"][:, SL[m]], ones_t[:], kv_t["I"][:, SL[m]],
                     carryI[:, m:m + 1], op0=OP.mult, op1=OP.add))
            step(v, "dve", s_dve, [f"cumR{m}", f"pqR{m}"], [f"ta{m}"],
                 lambda: v.scalar_tensor_tensor(ta[m][:], cum_t["R"][:, SL[m]], 1.0,
                                                proj["qR"][:, SL[m]],
                                                op0=OP.bypass, op1=OP.mult))
            step(v, "dve", s_dve, [f"cumI{m}", f"pqI{m}"], [f"tb{m}"],
                 lambda: v.scalar_tensor_tensor(tb[m][:], cum_t["I"][:, SL[m]], 1.0,
                                                proj["qI"][:, SL[m]],
                                                op0=OP.bypass, op1=OP.mult))
            step(v, "dve", s_dve, [f"ta{m}", f"tb{m}"], [f"zR{m}_{zp}"],
                 lambda: v.scalar_tensor_tensor(z_t["R"][:, SL[m]], ta[m][:], 1.0,
                                                tb[m][:], op0=OP.bypass, op1=OP.add))
            if m == 0:
                step(v, "dve", s_dve, ["ta0"], [f"zR0_{zp}"],
                     lambda: v.tensor_copy(z_t["R"][0:1, SL[0]], ta[0][0:1, :]))

        @block.gpsimd
        def _(g, m=m):
            step(g, "pool", s_pool, [f"cumI{m}", f"pqR{m}"], [f"tc{m}"],
                 lambda: g.tensor_mul(tc[m][:], cum_t["I"][:, SL[m]],
                                      proj["qR"][:, SL[m]]))
            step(g, "pool", s_pool, [f"cumR{m}", f"pqI{m}"], [f"td{m}"],
                 lambda: g.tensor_mul(td[m][:], cum_t["R"][:, SL[m]],
                                      proj["qI"][:, SL[m]]))
            step(g, "pool", s_pool, [f"tc{m}", f"td{m}"], [f"zI{m}_{zp}"],
                 lambda: g.tensor_tensor(z_t["I"][:, SL[m]], tc[m][:], td[m][:],
                                         op=OP.subtract))
            if m == 0:
                step(g, "pool", s_pool, ["tb0"], [f"zI0_{zp}"],
                     lambda: g.tensor_copy(z_t["I"][0:1, SL[0]], tb[0][0:1, :]))

    def emit_projwaves(st, is_a1, fkeys, mid_cb=None):
        fb = st % 2
        zp = st % 2
        z_t = z_t2[zp]
        plist = ("kR", "kI", "vR", "vI") if is_a1 else ("kR", "kI", "vR", "vI", "qR", "qI")
        P = len(plist)
        chains = [(pn, mb) for mb in range(NMB) for pn in plist]
        mid_group = 0 if is_a1 else 4
        ew_done = 0
        for gi in range(0, len(chains), 4):
            group = chains[gi:gi + 4]
            pend = []
            for pn, mb in group:
                reg = singles[sidx[0] % 4]
                sidx[0] += 1

                @block.tensor
                def _(te, pn=pn, mb=mb, reg=reg, fb=fb, fkeys=fkeys):
                    dep_wait(te, "pe", fkeys + ["A" + pn, "b" + pn, "ones1"], [])
                    pacq(te, reg)
                    for dk in range(NDK):
                        te.matmul(
                            reg.ap,
                            sA[pn][:, dk * NB + mb * 128: dk * NB + (mb + 1) * 128],
                            xn_f[fb][:, dk * TST:(dk + 1) * TST],
                            start=(dk == 0), stop=False)
                    inst = te.matmul(reg.ap,
                                     sbias[pn][0:1, mb * 128:(mb + 1) * 128],
                                     ones1_t[0:1, :],
                                     start=False, stop=True)
                    v = s_pe.inc(inst)
                    record(s_pe, v, fkeys + ["A" + pn, "b" + pn, "ones1"], [])
                pend.append((pn, mb, reg, s_pe.n))

            @block.scalar
            def _(sc, pend=pend):
                for pn, mb, reg, pe_done in pend:
                    wge(sc, "act", s_pe, pe_done)
                    step(sc, "act", s_act, [], [f"p{pn}{mb}"],
                         lambda pn=pn, mb=mb, reg=reg:
                         sc.copy(proj[pn][:, mb * TST:(mb + 1) * TST], reg.ap))
                    prel(reg, s_act, s_act.n)

            while ew_done < NMB and ew_done * P + P - 1 < gi + 4:
                if is_a1:
                    emit_ew_a1(ew_done)
                else:
                    emit_ew_a2(ew_done, zp, z_t)
                ew_done += 1
            if gi // 4 == mid_group and mid_cb is not None:
                mid_cb()
        assert ew_done == NMB

        if is_a1:
            @block.vector
            def _(v):
                ppR = par[("R", 0)]
                ppI = par[("I", 0)]
                step(v, "dve", s_dve, ["rta", "rtb"], ["racc0"],
                     lambda: v.tensor_sub(racc_t[:, 0:NMB], rred["ta"][:], rred["tb"][:]))
                step(v, "dve", s_dve, ["rta", "racc0"], ["racc0"],
                     lambda: v.tensor_copy(racc_t[0:1, 0:1], rred["ta"][0:1, 0:1]))
                step(v, "dve", s_dve, ["racc0", f"accR{ppR}"], [f"accR{1 - ppR}"],
                     lambda ppR=ppR: v.tensor_add(acc[("R", 1 - ppR)][:], acc[("R", ppR)][:],
                                                  racc_t[:, 0:NMB]))
                par[("R", 0)] = 1 - ppR
                step(v, "dve", s_dve, ["rtc", "rtd"], ["racc1"],
                     lambda: v.tensor_add(racc_t[:, NMB:2 * NMB], rred["tc"][:], rred["td"][:]))
                step(v, "dve", s_dve, ["rtb", "racc1"], ["racc1"],
                     lambda: v.tensor_copy(racc_t[0:1, NMB:NMB + 1], rred["tb"][0:1, 0:1]))
                step(v, "dve", s_dve, ["racc1", f"accI{ppI}"], [f"accI{1 - ppI}"],
                     lambda ppI=ppI: v.tensor_add(acc[("I", 1 - ppI)][:], acc[("I", ppI)][:],
                                                  racc_t[:, NMB:2 * NMB]))
                par[("I", 0)] = 1 - ppI
            return

        @block.vector
        def _(v):
            step(v, "dve", s_dve, [f"cumR{m}" for m in range(NMB)], ["carryR"],
                 lambda: v.tensor_copy(
                     carryR[:], bass.AP(cum_t["R"], TST - 1, [[NMB * TST, 128], [TST, NMB]])))
            step(v, "dve", s_dve, [f"cumI{m}" for m in range(NMB)], ["carryI"],
                 lambda: v.tensor_copy(
                     carryI[:], bass.AP(cum_t["I"], TST - 1, [[NMB * TST, 128], [TST, NMB]])))

    def emit_tail(st):
        zp = st % 2
        z_t = z_t2[zp]
        zkeys = [f"z{RI}{m}_{zp}" for RI in "RI" for m in range(NMB)]
        for tci in range(NTC):
            reg = doubles[didx[0] % 2]
            didx[0] += 1

            @block.tensor
            def _(te, tci=tci, reg=reg):
                dep_wait(te, "pe", zkeys + ["sG"], [])
                pacq(te, reg)
                for gk in range(8):
                    zt = z_t["R"] if gk < 4 else z_t["I"]
                    mbk = gk % 4
                    lhsT = zt[:, mbk * TST + tci * 128: mbk * TST + (tci + 1) * 128]
                    for h in range(2):
                        inst = te.matmul(reg.ap[:, h * 512:(h + 1) * 512], lhsT,
                                         sG[:, gk * D + h * 512: gk * D + (h + 1) * 512],
                                         start=(gk == 0), stop=(gk == 7))
                v = s_pe.inc(inst)
                record(s_pe, v, zkeys + ["sG"], [])
            ifft_done = s_pe.n
            ab = tci % 2

            @block.scalar
            def _(sc, reg=reg, ab=ab, tci=tci, ifft_done=ifft_done):
                wge(sc, "act", s_pe, ifft_done)
                step(sc, "act", s_act, [], [f"attn{ab}"],
                     lambda: sc.copy(attn_t[ab][:], reg.ap))
                step(sc, "act", s_act, [], [f"attn{1 - ab}", "S"],
                     lambda: sc.activation(attn_t[1 - ab][:], reg.ap, AF.Square,
                                           accum_out=S_t[:, tci:tci + 1]))
                step(sc, "act", s_act, ["S"], ["lnS"],
                     lambda: sc.activation(lnS_t[:, tci:tci + 1], S_t[:, tci:tci + 1], AF.Ln))
                step(sc, "act", s_act, ["lnS", "cst2"], ["sqS"],
                     lambda: sc.activation(sqS_t[:, tci:tci + 1], lnS_t[:, tci:tci + 1],
                                           AF.Exp, scale=cst_t[:, 2:3]))
            prel(reg, s_act, s_act.n - 2)

            pb = tci % 2

            @block.vector
            def _(v, tci=tci, ab=ab, pb=pb):
                step(v, "dve", s_dve, ["sqS"], ["inv1"],
                     lambda: v.tensor_scalar(inv_t[:, tci:tci + 1], sqS_t[:, tci:tci + 1],
                                             1024.0, 1e-9, op0=OP.mult, op1=OP.add))
                step(v, "dve", s_dve, ["inv1"], ["inv"],
                     lambda: v.reciprocal(inv_t[:, tci:tci + 1], inv_t[:, tci:tci + 1]))
                step(v, "dve", s_dve, [f"attn{ab}", "inv"], [f"part{pb}"],
                     lambda: v.tensor_scalar(part_t[pb][:], attn_t[ab][:],
                                             inv_t[:, tci:tci + 1], None, op0=OP.mult))

            @block.sync
            def _(sync, st=st, tci=tci, pb=pb):
                dstep(sync, "sp", s_pw[pb], [f"part{pb}"], [f"partial_dram{tci % 4}"],
                      lambda: sync.dma_start(
                          partial_dram.ap()[st * TST + tci * 128: st * TST + (tci + 1) * 128, :],
                          part_t[pb][:]))

    xload_sched = ([(st, x_extra) for st in range(NST)]
                   + [(st, x_own) for st in range(NST)])
    xl_i = [0]

    def emit_next_xload():
        if xl_i[0] < len(xload_sched):
            st, xd = xload_sched[xl_i[0]]
            xl_i[0] += 1
            emit_xload(st, xd)

    a_fk = {}
    emit_next_xload()
    emit_next_xload()
    for it in range(NST + 1):
        mid = None
        if it < NST:
            emit_prep_pre(it)
            mid = mk_prep_finish(it, None, None)
        else:
            emit_prep_pre(0)
            mid = mk_prep_finish(0, xn_dram, "xn_dram")
        if it >= 1:
            emit_projwaves(it - 1, True, a_fk[it - 1], mid)
        elif mid is not None:
            mid()
        emit_next_xload()
        emit_conv_tasks(4)

    @block.vector
    def _(v):
        ppR, ppI = par[("R", 0)], par[("I", 0)]
        step(v, "dve", s_dve, [f"accR{ppR}", "scl3"], ["carryR"],
             lambda ppR=ppR: v.tensor_scalar(
                 carryR[:], acc[("R", ppR)][:], s_scales[:, 3:4], None, op0=OP.mult))
        step(v, "dve", s_dve, [f"accI{ppI}", "scl3"], ["carryI"],
             lambda ppI=ppI: v.tensor_scalar(
                 carryI[:], acc[("I", ppI)][:], s_scales[:, 3:4], None, op0=OP.mult))

    for it in range(NST + 2):
        mid = None
        if it == 0:
            emit_prep_pre(1)
            mk_prep_finish(1, xn_dram, "xn_dram")()
        if 2 <= it < NST:
            emit_prep_pre(it)
            mid = mk_prep_finish(it, xn_dram, "xn_dram")
        if 1 <= it <= NST:
            emit_projwaves(it - 1, False, a_fk[it - 1], mid)
        if 1 <= it < NST:
            emit_next_xload()
        if it >= 2:
            emit_tail(it - 2)
        emit_conv_tasks(4)

    emit_conv_tasks(len(conv_tasks))
    assert conv_i[0] == len(conv_tasks)

    es_a.close()

    es_b = ExitStack()
    K0d_t = mk_sb(es_b, "K0d_t", [128, NDK * F0], BF16)
    r0_t = mk_sb(es_b, "r0_t", [128, NF0 * TST], BF16)
    r1_t = mk_sb(es_b, "r1_t", [128, NF1 * TST], BF16)
    w1ring = [mk_sb(es_b, f"w1r{i}", [128, F0], BF16) for i in range(3)]
    w2ring = mk_sb(es_b, "w2ring", [128, 4 * D], BF16)

    part_t.append(mk_sb(es_b, "part_t2", [128, D], BF16))
    part_t.append(mk_sb(es_b, "part_t3", [128, D], BF16))
    bx_t = [mk_sb(es_b, f"bx_t{i}", [128, D], F32) for i in range(4)]

    end_marks = [(s_pe, s_pe.n), (s_act, s_act.n), (s_dve, s_dve.n),
                 (s_pool, s_pool.n),
                 (s_xnw, s_xnw.n), (s_xnw2, s_xnw2.n), (s_wld, s_wld.n),
                 (s_wldb, s_wldb.n), (s_bld, s_bld.n)]
    end_marks += [(c, c.n) for c in s_x + s_cv + s_xnt + s_pw + s_cw]

    def barrier(eng, key):
        for c, v in end_marks:
            wge(eng, key, c, v)

    @block.sync
    def _(sync):
        barrier(sync, "sp")
        dstep(sync, "sp", s_k0d, ["k0s"], ["K0da"],
              lambda: sync.dma_start(
                  K0d_t[:, 0:3 * F0],
                  k0s.ap()[0:3 * 128, :].rearrange("(k p) m -> p k m", p=128)))

    @block.scalar
    def _(sc):
        barrier(sc, "act")
        dstep(sc, "act", s_xnw2, ["k0s"], ["K0db"],
              lambda: sc.dma_start(
                  K0d_t[:, 3 * F0:6 * F0],
                  k0s.ap()[3 * 128:6 * 128, :].rearrange("(k p) m -> p k m", p=128)))

    @block.gpsimd
    def _(g):
        barrier(g, "pool")
        dstep(g, "pool", s_bld, ["k0s"], ["K0dc"],
              lambda: g.dma_start(
                  K0d_t[:, 6 * F0:8 * F0],
                  k0s.ap()[6 * 128:8 * 128, :].rearrange("(k p) m -> p k m", p=128)))



    def emit_prep_b(j):
        fb = j % 2
        fkeys = [f"xnf{fb}_{dk}" for dk in range(NDK)]

        @block.sync
        def _(sync, j=j, fb=fb, fkeys=fkeys):
            for dk in range(NDK):
                dstep(sync, "sp", s_xnt[fb], ["xn_dram0"], [fkeys[dk]],
                      lambda dk=dk: sync.dma_start(
                          xn_f[fb][:, dk * TST:(dk + 1) * TST],
                          xn_dram.ap()[j * TST:(j + 1) * TST, dk * 128:(dk + 1) * 128],
                          transpose=True))
            for fk in fkeys:
                track[fk] = [(s_xnt[fb], s_xnt[fb].n), {}]
        return fkeys

    def emit_compute_b(j, fkeys):
        fb = j % 2
        for tci in range(NTC):
            @block.gpsimd
            def _(g, j=j, tci=tci):
                dstep(g, "pool", s_pr[tci], [f"partial_dram{tci}"], [f"bpart{tci}"],
                      lambda: g.dma_start(
                          part_t[tci][:],
                          partial_dram.ap()[j * TST + tci * 128: j * TST + (tci + 1) * 128, :]))
                dstep(g, "pool", s_bx[tci], [], [f"bx{tci}"],
                      lambda: g.dma_start(
                          bx_t[tci][:],
                          x_own.ap()[j * TST + tci * 128: j * TST + (tci + 1) * 128, :]))

        for g0 in range(0, NF0, 4):
            pend = []
            for m in range(g0, g0 + 4):
                reg = singles[sidx[0] % 4]
                sidx[0] += 1

                @block.tensor
                def _(te, m=m, reg=reg, fb=fb, fkeys=fkeys):
                    dep_wait(te, "pe", fkeys + ["K0da", "K0db", "K0dc"], [])
                    pacq(te, reg)
                    for dk in range(NDK):
                        inst = te.matmul(
                            reg.ap,
                            K0d_t[:, dk * F0 + m * 128: dk * F0 + (m + 1) * 128],
                            xn_f[fb][:, dk * TST:(dk + 1) * TST],
                            start=(dk == 0), stop=(dk == NDK - 1))
                    v = s_pe.inc(inst)
                    record(s_pe, v, fkeys + ["K0da", "K0db", "K0dc"], [])
                pend.append((m, reg, s_pe.n))

            @block.scalar
            def _(sc, pend=pend):
                for m, reg, pe_done in pend:
                    wge(sc, "act", s_pe, pe_done)
                    step(sc, "act", s_act, ["scl0"], [f"r0_{m}"],
                         lambda m=m, reg=reg: sc.activation(
                             r0_t[:, m * TST:(m + 1) * TST], reg.ap,
                             AF.Relu, scale=s_scales[:, 0:1]))
                    prel(reg, s_act, s_act.n)
        r0keys = [f"r0_{m}" for m in range(NF0)]

        for c in range(3):
            @block.gpsimd
            def _(g, c=c):
                wge(g, "pool", s_cw[0], s_cw[0].n)
                wge(g, "pool", s_cw[1], s_cw[1].n)
                dstep(g, "pool", s_w1[c % 3], ["w1s"], [f"w1r{c % 3}"],
                      lambda: g.dma_start(w1ring[c % 3][:], w1s.ap()[c]))

            @block.vector
            def _(v, c=c):
                step(v, "dve", s_dve, [f"w1r{c % 3}"], [f"w1r{c % 3}"],
                     lambda: v.tensor_scalar(w1ring[c % 3][:], w1ring[c % 3][:],
                                             2.0, -1.0, op0=OP.mult, op1=OP.add))
        for g0 in range(0, NF1, 4):
            pend = []
            for c in range(g0, g0 + 4):
                reg = singles[sidx[0] % 4]
                sidx[0] += 1

                @block.tensor
                def _(te, c=c, reg=reg):
                    dep_wait(te, "pe", [f"w1r{c % 3}"] + r0keys, [])
                    pacq(te, reg)
                    for k in range(NF0):
                        inst = te.matmul(
                            reg.ap,
                            w1ring[c % 3][:, k * 128:(k + 1) * 128],
                            r0_t[:, k * TST:(k + 1) * TST],
                            start=(k == 0), stop=(k == NF0 - 1))
                    v = s_pe.inc(inst)
                    record(s_pe, v, [f"w1r{c % 3}"] + r0keys, [])
                pend.append((c, reg, s_pe.n))

                if c + 3 < NF1:
                    @block.gpsimd
                    def _(g, c=c):
                        dstep(g, "pool", s_w1[c % 3], ["w1s"], [f"w1r{c % 3}"],
                              lambda: g.dma_start(w1ring[c % 3][:], w1s.ap()[c + 3]))

                    @block.vector
                    def _(v, c=c):
                        step(v, "dve", s_dve, [f"w1r{c % 3}"], [f"w1r{c % 3}"],
                             lambda: v.tensor_scalar(w1ring[c % 3][:], w1ring[c % 3][:],
                                                     2.0, -1.0, op0=OP.mult, op1=OP.add))

            @block.scalar
            def _(sc, pend=pend):
                for c, reg, pe_done in pend:
                    wge(sc, "act", s_pe, pe_done)
                    step(sc, "act", s_act, ["scl1"], [f"r1_{c}"],
                         lambda c=c, reg=reg: sc.activation(
                             r1_t[:, c * TST:(c + 1) * TST], reg.ap,
                             AF.Relu, scale=s_scales[:, 1:2]))
                    prel(reg, s_act, s_act.n)
        r1keys = [f"r1_{c}" for c in range(NF1)]

        for c in range(4):
            @block.gpsimd
            def _(g, c=c):
                dstep(g, "pool", s_w2[c % 4], [], [f"w2r{c % 4}"],
                      lambda: g.dma_start(w2ring[:, (c % 4) * D:(c % 4 + 1) * D],
                                          k2u.ap()[c * 128:(c + 1) * 128, :]))

            @block.vector
            def _(v, c=c):
                step(v, "dve", s_dve, [f"w2r{c % 4}"], [f"w2r{c % 4}"],
                     lambda: v.tensor_scalar(w2ring[:, (c % 4) * D:(c % 4 + 1) * D],
                                             w2ring[:, (c % 4) * D:(c % 4 + 1) * D],
                                             2.0, -1.0, op0=OP.mult, op1=OP.add))
        for c in range(NF1):
            @block.tensor
            def _(te, c=c):
                dep_wait(te, "pe", [f"w2r{c % 4}", f"r1_{c}"], [])
                if c == 0:
                    for reg in doubles:
                        pacq(te, reg)
                for tci in range(NTC):
                    reg = doubles[tci]
                    lhsT = r1_t[:, c * TST + tci * 128: c * TST + (tci + 1) * 128]
                    for h in range(2):
                        inst = te.matmul(
                            reg.ap[:, h * 512:(h + 1) * 512], lhsT,
                            w2ring[:, (c % 4) * D + h * 512:(c % 4) * D + (h + 1) * 512],
                            start=(c == 0), stop=(c == NF1 - 1))
                v = s_pe.inc(inst)
                record(s_pe, v, [f"w2r{c % 4}", f"r1_{c}"], [])
            if c + 4 < NF1:
                @block.gpsimd
                def _(g, c=c):
                    dstep(g, "pool", s_w2[c % 4], [], [f"w2r{c % 4}"],
                          lambda: g.dma_start(w2ring[:, (c % 4) * D:(c % 4 + 1) * D],
                                              k2u.ap()[(c + 4) * 128:(c + 5) * 128, :]))

                @block.vector
                def _(v, c=c):
                    step(v, "dve", s_dve, [f"w2r{c % 4}"], [f"w2r{c % 4}"],
                         lambda: v.tensor_scalar(w2ring[:, (c % 4) * D:(c % 4 + 1) * D],
                                                 w2ring[:, (c % 4) * D:(c % 4 + 1) * D],
                                                 2.0, -1.0, op0=OP.mult, op1=OP.add))
        mm2_done = s_pe.n

        for tci in (2, 3, 0, 1):
            reg = doubles[tci]

            @block.vector
            def _(v, tci=tci, reg=reg):
                wge(v, "dve", s_pe, mm2_done)
                step(v, "dve", s_dve, [f"bx{tci}", "scl2"], [f"bx{tci}"],
                     lambda: v.scalar_tensor_tensor(bx_t[tci][:], reg.ap,
                                                    s_scales[:, 2:3], bx_t[tci][:],
                                                    op0=OP.mult, op1=OP.add))
            prel(reg, s_dve, s_dve.n)
        for tci in range(NTC):
            @block.vector
            def _(v, tci=tci):
                step(v, "dve", s_dve, [f"bpart{tci}"], [f"bx{tci}"],
                     lambda: v.tensor_add(bx_t[tci][:], bx_t[tci][:], part_t[tci][:]))

            @block.scalar
            def _(sc, j=j, tci=tci):
                dstep(sc, "act", s_ow[tci], [f"bx{tci}"], [],
                      lambda: sc.dma_start(
                          out.ap()[j * TST + tci * 128: j * TST + (tci + 1) * 128, :],
                          bx_t[tci][:]))

    b_fk = {}
    for j in range(NST + 1):
        if j < NST:
            b_fk[j] = emit_prep_b(j)
        if j >= 1:
            emit_compute_b(j - 1, b_fk[j - 1])

    @block.sync
    def _(sync):
        for c in s_ow:
            sync.wait_ge(c.h, c.n)

    es_b.close()
    es.close()
    return nc


def kernel(L0_attn_x, L0_attn_Wk, L0_attn_bk, L0_attn_Wv, L0_attn_bv,
           L0_attn_Wq, L0_attn_bq, L0_ffn_k0, L0_ffn_k1, L0_ffn_k2,
           L0_ffn_s0, L0_ffn_s1, L0_ffn_s2):
    x = np.ascontiguousarray(np.asarray(L0_attn_x, dtype=np.float32))
    B, S, _ = x.shape
    cosm, sinm, G = host_consts()
    k0a = np.ascontiguousarray(np.asarray(L0_ffn_k0)).view(np.uint8)
    k1a = np.ascontiguousarray(np.asarray(L0_ffn_k1)).view(np.uint8)
    k2a = np.ascontiguousarray(np.asarray(L0_ffn_k2)).view(np.uint8)

    def rep(v):
        return np.ascontiguousarray(
            np.broadcast_to(np.asarray(v, np.float32).reshape(1, 1), (128, 1)))

    common = {
        "Wk": np.ascontiguousarray(np.asarray(L0_attn_Wk, np.float32)),
        "Wv": np.ascontiguousarray(np.asarray(L0_attn_Wv, np.float32)),
        "Wq": np.ascontiguousarray(np.asarray(L0_attn_Wq, np.float32)),
        "bk": np.ascontiguousarray(np.asarray(L0_attn_bk, np.float32)),
        "bv": np.ascontiguousarray(np.asarray(L0_attn_bv, np.float32)),
        "bq": np.ascontiguousarray(np.asarray(L0_attn_bq, np.float32)),
        "k0u": k0a, "k1u": k1a, "k2u": k2a,
        "s0r": rep(L0_ffn_s0), "s1r": rep(L0_ffn_s1), "s2r": rep(L0_ffn_s2),
        "cosm": cosm, "sinm": sinm, "Gm": G,
        "identm": np.ascontiguousarray(np.eye(128, dtype=np.float32).astype(BF)),
    }
    in_maps = []
    for c in range(8):
        b, h = c // 2, c % 2
        im = dict(common)
        im["x_own"] = np.ascontiguousarray(x[b, h * TC:(h + 1) * TC])
        im["x_extra"] = np.ascontiguousarray(x[b, 0:TC])
        im["maskr"] = np.ascontiguousarray(
            np.full((128, 1), 1.0 if h == 1 else 0.0, np.float32))
        in_maps.append(im)

    nc = build_nc()
    res = run_bass_kernel_spmd(nc, in_maps, core_ids=list(range(8)))
    outp = np.empty((B, S, D), np.float32)
    for c in range(8):
        b, h = c // 2, c % 2
        outp[b, h * TC:(h + 1) * TC] = res.results[c]["out"]
    return outp
